# version 23
# baseline (speedup 1.0000x reference)
import os
os.environ.setdefault("GAUGE_NTFF_IGNORE_DMA", "1")

import numpy as np
from contextlib import ExitStack
from collections import deque

import concourse.bass as bass
import concourse.mybir as mybir
from concourse.bass_utils import run_bass_kernel_spmd

F32 = mybir.dt.float32
BF16 = mybir.dt.bfloat16
FP16 = mybir.dt.float16
ALU = mybir.AluOpType
AF = mybir.ActivationFunctionType
AX = mybir.AxisListType

D = 1024
T = 4096
TC = 256
NCH = T // TC
W = 8
NWIN = TC // W
QD = 8
NST = 32
DN = QD * NST
EPS = 1e-5

_offs = {}
_ck = 0
_cw = {"ipb": 16, "convw": 32, "convb": 8, "dtbias": 8, "outb": 8, "lng": 8,
       "lnb": 8, "alog": 256, "w1s": 32, "w2s": 32, "h1s": 32, "h2s": 32,
       "w3s": 8, "bcb": 64}
for _n, _w in _cw.items():
    _offs[_n] = _ck
    _ck += _w
CK = _ck

DEBUG_TAPS = {}
FFN_INTERLEAVE = 3


def build_program():
    nc = bass.Bass("TRN2", target_bir_lowering=False)

    xT = nc.dram_tensor("xT", [D, T], F32, kind="ExternalInput").ap()
    consts_d = nc.dram_tensor("consts", [128, CK], F32, kind="ExternalInput").ap()
    win = {}
    for nm, shp in [("ipT", [D, 2 * D]), ("dtT", [D, D]), ("outT", [D, D]),
                    ("bcT", [D, 64]), ("w1T", [D, 4 * D]), ("w2T", [D, 4 * D]),
                    ("h1T", [4 * D, 4 * D]), ("h2T", [4 * D, 4 * D]),
                    ("w3T", [4 * D, D])]:
        win[nm] = nc.dram_tensor(nm, shp, F32, kind="ExternalInput").ap()
    wbf = {nm: nc.dram_tensor(nm + "b", list(win[nm].shape), BF16, kind="Internal").ap()
           for nm in win}
    oT = nc.dram_tensor("oT", [D, T], F32, kind="ExternalOutput").ap()
    dbg_t = {}
    for _dn in DEBUG_TAPS:
        dbg_t[_dn] = None
    btct = [nc.dram_tensor(f"btct{i}", [TC * 64], F32, kind="Internal").ap()
            for i in range(2)]
    statscr = [nc.dram_tensor(f"stsc{i}", [520], F32, kind="Internal").ap()
               for i in range(4)]

    es = ExitStack()
    with es:
        def sb(name, shape, dt=F32):
            return es.enter_context(nc.sbuf_tensor(name, shape, dt))

        consts = sb("consts_sb", [128, CK])
        Ap = sb("Ap_sb", [128, DN])
        ones_bf = sb("ones_bf", [128, 1], BF16)
        ones_row = sb("ones_row", [1, 128])
        cap10 = sb("cap10", [128, 1])
        ncap10 = sb("ncap10", [128, 1])
        c_eps = sb("c_eps", [128, 1])
        c_one = sb("c_one", [128, 1])
        X = sb("X", [128, QD * (TC + 1)])
        xnb = sb("xnb", [128, QD * (TC + 1)], BF16)
        xc = sb("xc", [128, QD * (TC + 3)])
        halo = sb("halo", [128, QD * 2])
        zb = sb("zb", [128, QD * TC])
        szh = sb("szh", [128, QD * TC], FP16)
        xs = sb("xs", [128, QD * TC], FP16)
        xsb = sb("xsb", [128, QD * TC], BF16)
        xsc = sb("xsc", [128, QD * TC], BF16)
        sw = sb("sw", [128, 8 * TC], FP16)
        sa_s = sb("sa_s", [128, DN])
        raw_s = sb("raw_s", [128, DN])
        acc_s = sa_s
        dtb_t = sb("dtb_t", [128, TC * QD], FP16)
        dxb_t = sb("dxb_t", [128, TC * QD], FP16)
        chist = sb("chist", [128, W * DN])
        amb = [sb(f"amb{i}", [128, W * DN], FP16) for i in range(3)]
        abar = [sb(f"abar{i}", [128, W * DN]) for i in range(3)]
        ub = [sb(f"ub{i}", [128, W * DN], FP16) for i in range(2)]
        brep = [sb(f"brep{i}", [128, W * NST]) for i in range(2)]
        crep = [sb(f"crep{i}", [128, W * NST]) for i in range(2)]
        hb = sb("hb", [128, 32 * TC], BF16)
        g1b = sb("g1b", [128, 32 * TC], BF16)
        x2 = sb("x2", [128, QD * TC])
        fb = sb("fb", [128, QD * TC], BF16)
        Fb = sb("Fb", [128, QD * TC])
        G = sb("G", [128, QD * (TC + 1)])
        stat = sb("stat", [128, 784])
        ysw = sb("ysw", [128, W * QD])
        bcsb = sb("bcsb", [128, 64])
        stat_cast = sb("stat_cast", [128, QD * (TC + 1)], BF16)
        slabm = [sb(f"slabm{i}", [128, QD * 128], BF16) for i in range(2)]
        slabf = [sb(f"slabf{i}", [128, 16 * 128], BF16) for i in range(4)]
        bcw = sb("bcw", [128, QD * 64], BF16)

        pb = [es.enter_context(nc.psum_tensor(f"pb{i}", [128, 512], F32))
              for i in range(8)]

        sems = {}
        for snm in ["v", "p", "s", "t", "st", "dtap",
                    "qm0", "qm1", "qf0", "qf1", "qf2", "qf3", "qx", "qbt0", "qbt1",
                    "wc0", "wc1", "wc2", "wc3", "wc4", "wc5", "wc6", "wc7",
                    "qci0", "qci1", "qco0", "qco1", "qmisc",
                    "sscr", "srep0", "srep1", "sbr0", "sbr1", "scr0", "scr1"]:
            sems[snm] = [es.enter_context(nc.semaphore(snm)), 0]

        block = es.enter_context(nc.Block())

        prog = {k: [] for k in "QVPST"}
        lastw = {}

        def dep(key, cname, val):
            if val is None or val <= 0:
                return
            if lastw.get((key, cname), 0) >= val:
                return
            lastw[(key, cname)] = val
            sem = sems[cname][0]
            prog[key].append(lambda e, s=sem, v=val: e.wait_ge(s, v))

        def run(key, fn, inc=None, inck=1):
            if inc is None:
                prog[key].append(fn)
                return None
            sems[inc][1] += inck
            v = sems[inc][1]
            sem = sems[inc][0]

            def g(e, fn=fn, s=sem, k=inck):
                fn(e).then_inc(s, k)
            prog[key].append(g)
            return v

        def cur(cname):
            return sems[cname][1]

        bank = [{"fill": ("t", 0), "free": ("v", 0)} for _ in range(8)]

        def bank_consumed(bi, cname, val):
            bank[bi]["free"] = (cname, val)

        def mm_group(bi, pairs, out_ap, deps=()):
            fc, fv = bank[bi]["free"]
            dep("T", fc, fv)
            for cn, v in deps:
                dep("T", cn, v)
            n = len(pairs)
            tv = None
            for i, (lf, rf) in enumerate(pairs):
                a, b = (i == 0), (i == n - 1)
                fn = (lambda e, lf=lf, rf=rf, o=out_ap, a=a, b=b:
                      e.matmul(out=o, lhsT=lf(), rhs=rf(), start=a, stop=b))
                if b:
                    tv = run("T", fn, inc="t")
                else:
                    run("T", fn)
            bank[bi]["fill"] = ("t", tv)
            return tv

        def tap(name, c_now, src_ap, dtype, deplist):
            if DEBUG_TAPS.get(name) != c_now:
                return
            shp = list(src_ap.shape)
            h = nc.dram_tensor("dbg_" + name, shp, dtype, kind="ExternalOutput").ap()
            for cn, v in deplist:
                dep("Q", cn, v)
            run("Q", lambda e, h=h, s=src_ap: e.dma_start(out=h, in_=s),
                inc="dtap", inck=16)
            for k in "VPSTQ":
                dep(k, "dtap", cur("dtap"))

        def cv(name, i=None):
            o = _offs[name]
            if i is None:
                return consts[:, o:o + _cw[name]]
            return consts[:, o + i:o + i + 1]

        def qv(t_ap, ncol):
            return t_ap.rearrange("p (q j) -> p q j", j=ncol)

        slabm_free = [("t", 0), ("t", 0)]
        slabf_free = [("t", 0)] * 4
        ffn_rr = [0]
        w12_pend = deque()
        sw_v = [0, 0, 0, 0]

        def load_slab(buf_list, free_list, slot, src_ap, deps=()):
            fc, fv = free_list[slot]
            dep("Q", fc, fv)
            for cn, v in deps:
                dep("Q", cn, v)
            cls = (("qm0", "qm1") if buf_list is slabm
                   else ("qf0", "qf1", "qf2", "qf3"))
            ncols = src_ap.size() // 128
            v = run("Q", lambda e, b=buf_list[slot], s=src_ap, nn=ncols:
                    e.dma_start(out=b[:, 0:nn], in_=s), inc=cls[slot], inck=16)
            return (cls[slot], v)

        run("Q", lambda e: e.dma_start(out=consts[:], in_=consts_d),
            inc="qmisc", inck=16)
        ldc = cur("qmisc")
        dep("V", "qmisc", ldc)
        run("V", lambda e: e.memset(ones_bf[:], 1.0))
        run("V", lambda e: e.memset(ones_row[:], 1.0))
        run("V", lambda e: e.memset(cap10[:], 10.0))
        run("V", lambda e: e.memset(ncap10[:], -10.0))
        run("V", lambda e: e.memset(c_eps[:], EPS))
        run("V", lambda e: e.memset(c_one[:], 1.0))
        run("V", lambda e: e.memset(chist[:, (W - 1) * DN:W * DN], 0.0))
        run("V", lambda e: e.memset(halo[:], 0.0))
        run("V", lambda e: e.tensor_scalar(out=G[:, 0:DN], in0=cv("alog"),
                                           scalar1=2.0, scalar2=-5.0,
                                           op0=ALU.min, op1=ALU.max))
        v_init = run("V", lambda e: e.engine_nop(), inc="v")
        dep("S", "v", v_init)
        s_ap = run("S", lambda e: e.activation(out=Ap[:], in_=G[:, 0:DN],
                                               func=AF.Exp), inc="s")
        dep("P", "s", s_ap)
        dep("T", "v", v_init)

        cast_dma = ["bcT", "ipT", "dtT", "outT", "w1T", "w2T", "w3T", "h2T"]
        cast_ffn = [("h1T", 4 * D, 4 * D, "w2s")]
        ci = [hb[:, 0:4096].bitcast(F32), hb[:, 4096:8192].bitcast(F32)]
        co = [g1b[:, 0:4096].bitcast(BF16), g1b[:, 4096:8192].bitcast(BF16)]
        cast_cp = [("v", 0), ("v", 0)]
        cast_out = [("qco0", 0), ("qco1", 0)]
        cast_pi = [0]
        cast_done_ld = {}

        def emit_cast_tile(nm, src, dst, r0, c0, cc, scnm, is_last):
            pi = cast_pi[0]
            b = pi % 2
            qci = "qci0" if b == 0 else "qci1"
            qco = "qco0" if b == 0 else "qco1"
            cn, vv0 = cast_cp[b]
            dep("Q", cn, vv0)
            ldv = run("Q", lambda e, s=src, r0=r0, c0=c0, cc=cc, b=b:
                      e.dma_start(out=ci[b][:, 0:cc],
                                  in_=s[r0:r0 + 128, c0:c0 + cc]),
                      inc=qci, inck=16)
            key = "S" if (scnm is not None or pi % 2 == 0) else "V"
            dep(key, qci, ldv)
            cn2, vv2 = cast_out[b]
            dep(key, cn2, vv2)
            if scnm is not None:
                kq = r0 // 128
                cpv = run("S", lambda e, b=b, cc=cc, col=cv(scnm, kq):
                          e.activation(out=co[b][:, 0:cc], in_=ci[b][:, 0:cc],
                                       func=AF.Identity, scale=col), inc="s")
                cpc = "s"
            elif key == "S":
                cpv = run("S", lambda e, b=b, cc=cc:
                          e.copy(out=co[b][:, 0:cc], in_=ci[b][:, 0:cc]),
                          inc="s")
                cpc = "s"
            else:
                cpv = run("V", lambda e, b=b, cc=cc:
                          e.tensor_copy(out=co[b][:, 0:cc], in_=ci[b][:, 0:cc]),
                          inc="v")
                cpc = "v"
            cast_cp[b] = (cpc, cpv)
            dep("Q", cpc, cpv)
            ldo = run("Q", lambda e, d=dst, r0=r0, c0=c0, cc=cc, b=b:
                      e.dma_start(out=d[r0:r0 + 128, c0:c0 + cc],
                                  in_=co[b][:, 0:cc]), inc=qco, inck=16)
            cast_out[b] = (qco, ldo)
            cast_pi[0] += 1
            if is_last:
                cast_done_ld[nm] = [("qco0", cur("qco0")), ("qco1", cur("qco1"))]

        def tiles_of(nm, R, C, scnm):
            src, dst = win[nm], wbf[nm]
            tl = [(r0, c0, min(2048, C - c0))
                  for r0 in range(0, R, 128) for c0 in range(0, C, 2048)]
            return [(nm, src, dst, r0, c0, cc, scnm, i == len(tl) - 1)
                    for i, (r0, c0, cc) in enumerate(tl)]

        for i, nm in enumerate(cast_dma):
            wcl = f"wc{i}"
            v = run("P", lambda e, nm=nm: e.dma_start(
                out=wbf[nm].rearrange("a b -> (a b)"),
                in_=win[nm].rearrange("a b -> (a b)")), inc=wcl, inck=16)
            cast_done_ld[nm] = [(wcl, v)]

        cast_tasks = deque()
        for nm, R, C, scnm in cast_ffn:
            for args in tiles_of(nm, R, C, scnm):
                cast_tasks.append(lambda a=args: emit_cast_tile(*a))

        for cn_, v_ in cast_done_ld["bcT"]:
            dep("Q", cn_, v_)
        dep("Q", "qmisc", ldc)
        bcldv = run("Q", lambda e: e.dma_start(
            out=bcw[:], in_=wbf["bcT"].rearrange("(q p) j -> p q j", p=128)),
            inc="qmisc", inck=16)
        bcld_cls = "qmisc" 

        ln_ring = [0]
        scr_ring = [0]
        last_d1 = [0]
        last_vp = [0]
        scr_free = [("v", 0)] * 4
        rep_free = [("v", 0)] * 2

        ln_state = {}

        def emit_ln(src_fn, ncol, cast_buf, deps=()):
            for cn, v in deps:
                dep("S", cn, v)
            dep("S", "v", last_vp[0])
            c1 = run("S", lambda e: e.copy(
                out=cast_buf[:, 0:QD * ncol].rearrange("p (q j) -> p q j", j=ncol),
                in_=src_fn()), inc="s")
            fc, fv = bank[5]["free"]
            dep("T", fc, fv)
            dep("T", "s", c1)
            t1 = None
            for q in range(QD):
                a, bl = (q == 0), (q == QD - 1)
                fn = (lambda e, q=q, ncol=ncol, cb=cast_buf, a=a, bl=bl:
                      e.matmul(out=pb[5][0:1, 0:ncol], lhsT=ones_bf[:],
                               rhs=cb[:, q * ncol:(q + 1) * ncol], start=a, stop=bl))
                t1 = run("T", fn, inc="t") if bl else run("T", fn)
            bank[5]["fill"] = ("t", t1)
            dep("S", "t", t1)
            c2 = run("S", lambda e: e.activation(
                out=cast_buf[:, 0:QD * ncol].rearrange("p (q j) -> p q j", j=ncol),
                in_=src_fn(), func=AF.Square), inc="s")
            fc, fv = bank[4]["free"]
            dep("T", fc, fv)
            dep("T", "s", c2)
            t2 = None
            for q in range(QD):
                a, bl = (q == 0), (q == QD - 1)
                fn = (lambda e, q=q, ncol=ncol, cb=cast_buf, a=a, bl=bl:
                      e.matmul(out=pb[4][0:1, 0:ncol], lhsT=ones_bf[:],
                               rhs=cb[:, q * ncol:(q + 1) * ncol], start=a, stop=bl))
                t2 = run("T", fn, inc="t") if bl else run("T", fn)
            bank[4]["fill"] = ("t", t2)
            dep("S", "t", t1)
            s_mu = run("S", lambda e, ncol=ncol: e.activation(
                out=stat[0:1, 0:ncol], in_=pb[5][0:1, 0:ncol],
                func=AF.Identity, scale=1.0 / D), inc="s")
            bank_consumed(5, "s", s_mu)
            dep("S", "s", s_mu)
            s_m2 = run("S", lambda e, ncol=ncol: e.activation(
                out=stat[32:33, 0:ncol], in_=stat[0:1, 0:ncol], func=AF.Square),
                inc="s")
            ln_state["t2"] = t2
            ln_state["s_m2"] = s_m2
            ln_state["ncol"] = ncol

        def emit_ln_fin():
            ncol = ln_state["ncol"]
            dep("V", "t", ln_state["t2"])
            dep("V", "s", ln_state["s_m2"])
            v_var = run("V", lambda e, ncol=ncol: e.scalar_tensor_tensor(
                out=stat[64:65, 0:ncol], in0=pb[4][0:1, 0:ncol],
                scalar=1.0 / D, in1=stat[32:33, 0:ncol],
                op0=ALU.mult, op1=ALU.subtract), inc="v")
            bank_consumed(4, "v", v_var)
            dep("S", "v", v_var)
            s_l = run("S", lambda e, ncol=ncol: e.activation(
                out=stat[96:97, 0:ncol], in_=stat[64:65, 0:ncol], func=AF.Ln,
                bias=c_eps[0:1, :]), inc="s")
            dep("S", "s", s_l)
            s_r = run("S", lambda e, ncol=ncol: e.activation(
                out=stat[0:1, 272:272 + ncol], in_=stat[96:97, 0:ncol], func=AF.Exp,
                scale=-0.5), inc="s")
            dep("V", "s", s_r)
            v_p = run("V", lambda e, ncol=ncol: e.scalar_tensor_tensor(
                out=stat[0:1, 528:528 + ncol], in0=stat[0:1, 0:ncol],
                scalar=-1.0, in1=stat[0:1, 272:272 + ncol],
                op0=ALU.mult, op1=ALU.mult), inc="v")
            fc, fv = bank[5]["free"]
            dep("T", fc, fv)
            dep("T", "s", s_r)
            tr_ = run("T", lambda e, ncol=ncol: e.matmul(
                out=pb[5][:, 0:ncol], lhsT=ones_row[:],
                rhs=stat[0:1, 272:272 + ncol], start=True, stop=True), inc="t")
            bank[5]["fill"] = ("t", tr_)
            fc, fv = bank[4]["free"]
            dep("T", fc, fv)
            dep("T", "v", v_p)
            tp_ = run("T", lambda e, ncol=ncol: e.matmul(
                out=pb[4][:, 0:ncol], lhsT=ones_row[:],
                rhs=stat[0:1, 528:528 + ncol], start=True, stop=True),
                inc="t")
            bank[4]["fill"] = ("t", tp_)
            last_vp[0] = v_p
            return tp_

        def rep_views(_, ncol):
            rstd = pb[5][:, 0:ncol].unsqueeze(1).broadcast_to([128, QD, ncol])
            p_ = pb[4][:, 0:ncol].unsqueeze(1).broadcast_to([128, QD, ncol])
            return rstd, p_

        HB = 136
        X_free_h = [("v", 0), ("v", 0)]
        xc_free = ("v", 0)
        xnb_free_h = [("t", 0), ("t", 0)]
        ygb_wfree = ("t", 0)
        fb_wfree = ("t", 0)
        out_dma_prev = ("st", 0)
        hb_ready = ("v", 0)
        zb_free = ("v", 0)
        x2b_free_h = [("v", 0), ("v", 0)]
        G_s = [0]
        G_v = [0]
        win_vfree = {}
        gw = [0]
        vdt_map = {}
        szh_map = {}
        btct_map = {}
        gate_map = {}
        ipb_cols = [cv("ipb", i) for i in range(16)]
        side = deque()
        ffn_tasks = deque()
        cur_ffn_chunk = [0]
        x2scr = [nc.dram_tensor(f"x2scr{i}", [128, QD * TC], F32, kind="Internal").ap()
                 for i in range(2)]
        x2_reload_map = {}

        def vh(fn):
            fn.vheavy = True
            return fn

        def tslice(buf, ncol_layout, t0, t1):
            return buf[:].rearrange("p (q j) -> p q j", j=ncol_layout)[:, :, t0:t1]

        ffn_ok = [False]

        def w1w2_group(cc, olist):
            def f():
                nonlocal hb_ready, fb_wfree
                cur_ffn_chunk[0] = cc
                t1 = t2 = 0

                def flush_pend():
                    if not w12_pend:
                        return 0
                    po, psv = w12_pend.popleft()
                    pso = po % 4
                    dep("V", "s", psv)
                    hv = run("V", lambda e, po=po, pso=pso: e.tensor_tensor(
                        out=hb[:, po * TC:(po + 1) * TC],
                        in0=sw[:, pso * TC:(pso + 1) * TC],
                        in1=sw[:, (4 + pso) * TC:(5 + pso) * TC],
                        op=ALU.mult), inc="v")
                    sw_v[pso] = hv
                    return hv

                if cc == 0:
                    dep("V", "qci0", cur("qci0"))
                    dep("V", "qci1", cur("qci1"))
                dep("T", "v", cur("v"))
                for o in olist:
                    so = o % 2
                    ld1 = load_slab(slabm, slabm_free, so,
                                    wbf["w1T"].rearrange("(k p) j -> p k j", p=128)
                                    [:, :, o * 128:(o + 1) * 128],
                                    deps=cast_done_ld["w1T"])
                    sl2 = ffn_rr[0] % 4
                    ffn_rr[0] += 1
                    ld2 = load_slab(slabf, slabf_free, sl2,
                                    wbf["w2T"].rearrange("(k p) j -> p k j", p=128)
                                    [:, :, o * 128:(o + 1) * 128],
                                    deps=cast_done_ld["w2T"])
                    pairs1 = [((lambda so=so, k=k: slabm[so][:, k * 128:(k + 1) * 128]),
                               (lambda k=k: fb[:, k * TC:(k + 1) * TC]))
                              for k in range(QD)]
                    t1 = mm_group(0, pairs1, pb[0][:, 0:TC], deps=[ld1])
                    slabm_free[so] = ("t", t1)
                    pairs2 = [((lambda sl2=sl2, k=k:
                                slabf[sl2][:, k * 128:(k + 1) * 128]),
                               (lambda k=k: fb[:, k * TC:(k + 1) * TC]))
                              for k in range(QD)]
                    t2 = mm_group(1, pairs2, pb[1][:, 0:TC], deps=[ld2])
                    slabf_free[sl2] = ("t", t2)
                    dep("S", "t", t1)
                    dep("S", "v", sw_v[o % 4])
                    run("S", lambda e, o=o: e.activation(
                        out=sw[:, (o % 4) * TC:(o % 4 + 1) * TC],
                        in_=pb[0][:, 0:TC], func=AF.Silu, scale=cv("w1s", o)),
                        inc="s")
                    dep("S", "t", t2)
                    sv = run("S", lambda e, o=o: e.activation(
                        out=sw[:, (4 + o % 4) * TC:(5 + o % 4) * TC],
                        in_=pb[1][:, 0:TC], func=AF.Identity), inc="s")
                    bank_consumed(0, "s", sv)
                    bank_consumed(1, "s", sv)
                    if len(w12_pend) == 4:
                        flush_pend()
                    w12_pend.append((o, sv))
                if olist[-1] == 31:
                    hv = 0
                    while w12_pend:
                        hv = flush_pend()
                    hb_ready = ("v", hv)
                    fb_wfree = ("t", max(t1, t2))
                    ffn_ok[0] = True
            return f

        HB_BANKS = (2, 3, 7)

        def mk_ffn_tasks(cc, cell):
            tasks = deque()

            def h_round(o, wnm, snm, cellkey):
                def f():
                    bi = HB_BANKS[o % 3]
                    fc, fv = bank[bi]["free"]
                    slots, lds = [], []
                    for half in range(2):
                        sl = ffn_rr[0] % 4
                        ffn_rr[0] += 1
                        slots.append(sl)
                        lds.append(load_slab(
                            slabf, slabf_free, sl,
                            wbf[wnm].rearrange("(k p) j -> p k j", p=128)
                            [:, half * 16:(half + 1) * 16, o * 128:(o + 1) * 128],
                            deps=cast_done_ld[wnm]))
                    dep("T", fc, fv)
                    if wnm == "h1T":
                        dep("T", "v", hb_ready[1])
                    else:
                        dep("T", "s", cell["g1_s"])
                    tv = None
                    rhs_buf = hb if wnm == "h1T" else g1b
                    for half in range(2):
                        dep("T", lds[half][0], lds[half][1])
                        for k in range(16):
                            kk = half * 16 + k
                            a = (kk == 0)
                            bl = (kk == 31)
                            fn = (lambda e, kk=kk, sl=slots[half], bi=bi, a=a, bl=bl,
                                  rb=rhs_buf, k=k:
                                  e.matmul(out=pb[bi][:, 0:TC],
                                           lhsT=slabf[sl][:, k * 128:(k + 1) * 128],
                                           rhs=rb[:, kk * TC:(kk + 1) * TC],
                                           start=a, stop=bl))
                            tv = run("T", fn, inc="t") if bl else run("T", fn)
                    slabf_free[slots[0]] = ("t", tv)
                    slabf_free[slots[1]] = ("t", tv)
                    bank[bi]["fill"] = ("t", tv)
                    cell[cellkey] = max(cell[cellkey], tv)
                    dep("S", "t", tv)
                    if cc == 0 and wnm == "h1T" and o == 0:
                        dep("S", "qco0", cur("qco0"))
                        dep("S", "qco1", cur("qco1"))
                    if wnm == "h2T" and o == 0:
                        dep("S", "t", cell["h1_t"])
                    dst = g1b if wnm == "h1T" else hb
                    sv = run("S", lambda e, o=o, bi=bi, snm=snm, dst=dst:
                             e.activation(
                        out=dst[:, o * TC:(o + 1) * TC], in_=pb[bi][:, 0:TC],
                        func=AF.Gelu, scale=cv(snm, o)), inc="s")
                    bank_consumed(bi, "s", sv)
                    if o == 31:
                        cell["g1_s" if wnm == "h1T" else "g2_s"] = sv
                return f

            def w3_round(o):
                def f():
                    bi = HB_BANKS[o % 3]
                    fc, fv = bank[bi]["free"]
                    slots, lds = [], []
                    for half in range(2):
                        sl = ffn_rr[0] % 4
                        ffn_rr[0] += 1
                        slots.append(sl)
                        lds.append(load_slab(
                            slabf, slabf_free, sl,
                            wbf["w3T"].rearrange("(k p) j -> p k j", p=128)
                            [:, half * 16:(half + 1) * 16, o * 128:(o + 1) * 128],
                            deps=cast_done_ld["w3T"]))
                    dep("T", fc, fv)
                    dep("T", "s", cell["g2_s"])
                    tv = None
                    for half in range(2):
                        dep("T", lds[half][0], lds[half][1])
                        for k in range(16):
                            kk = half * 16 + k
                            a = (kk == 0)
                            bl = (kk == 31)
                            fn = (lambda e, kk=kk, sl=slots[half], bi=bi, a=a, bl=bl,
                                  k=k:
                                  e.matmul(out=pb[bi][:, 0:TC],
                                           lhsT=slabf[sl][:, k * 128:(k + 1) * 128],
                                           rhs=hb[:, kk * TC:(kk + 1) * TC],
                                           start=a, stop=bl))
                            tv = run("T", fn, inc="t") if bl else run("T", fn)
                    slabf_free[slots[0]] = ("t", tv)
                    slabf_free[slots[1]] = ("t", tv)
                    bank[bi]["fill"] = ("t", tv)
                    dep("S", "t", tv)
                    if o == 0:
                        cn, vv = out_dma_prev
                        dep("S", cn, vv)
                    sv = run("S", lambda e, o=o, bi=bi: e.activation(
                        out=Fb[:, o * TC:(o + 1) * TC], in_=pb[bi][:, 0:TC],
                        func=AF.Identity, scale=cv("w3s", o)), inc="s")
                    bank_consumed(bi, "s", sv)
                return f

            for o in range(32):
                tasks.append(h_round(o, "h1T", "h1s", "h1_t"))
            for o in range(32):
                tasks.append(h_round(o, "h2T", "h2s", "h2_t"))
            for o in range(QD):
                tasks.append(w3_round(o))
            return tasks

        def ln_apply_to_G(src_fn, ncol, t0, t1, out2=None):
            n = t1 - t0
            rstd = pb[5][:, 0:n].unsqueeze(1).broadcast_to([128, QD, n])
            p_ = pb[4][:, 0:n].unsqueeze(1).broadcast_to([128, QD, n])
            dep("V", "s", G_s[0])
            run("V", lambda e: e.tensor_tensor(
                out=tslice(G, TC + 1, t0, t1), in0=src_fn(), in1=rstd, op=ALU.mult))
            dst = out2 if out2 is not None else (lambda: tslice(G, TC + 1, t0, t1))
            vv = run("V", lambda e: e.tensor_tensor(
                out=dst(), in0=tslice(G, TC + 1, t0, t1),
                in1=p_, op=ALU.add), inc="v")
            G_v[0] = vv
            return vv

        def post_half_closures(c, h):
            t0, t1 = (0, HB) if h == 0 else (HB, TC)
            n = t1 - t0
            cls = []

            def c_outproj_a(opart):
                def f():
                    nonlocal ygb_wfree
                    dep_v = gate_map[(c, h)]
                    op_t = 0
                    for o in opart:
                        so = o % 2
                        ldv = load_slab(slabm, slabm_free, so,
                                        wbf["outT"].rearrange("(k p) j -> p k j", p=128)
                                        [:, :, o * 128:(o + 1) * 128],
                                        deps=cast_done_ld["outT"])
                        bi = o % 2
                        pairs = [((lambda so=so, k=k: slabm[so][:, k * 128:(k + 1) * 128]),
                                  (lambda k=k, t0=t0, t1=t1:
                                   xsb[:, k * TC + t0:k * TC + t1]))
                                 for k in range(QD)]
                        mmdeps = [ldv, ("v", dep_v)]
                        if o == 0 and h == 0:
                            mmdeps.append(x2b_free_h[0])
                        if o == 0 and h == 1:
                            mmdeps.append(x2b_free_h[1])
                        tv = mm_group(bi, pairs, pb[bi][:, 0:n], deps=mmdeps)
                        slabm_free[so] = ("t", tv)
                        op_t = tv
                        dep("S", "t", tv)
                        cn, vv = x2b_free_h[h]
                        dep("S", cn, vv)
                        sv = run("S", lambda e, o=o, bi=bi, t0=t0, t1=t1, n=n:
                                 e.activation(
                            out=x2[:, o * TC + t0:o * TC + t1], in_=pb[bi][:, 0:n],
                            func=AF.Identity, bias=cv("outb", o)), inc="s")
                        bank_consumed(bi, "s", sv)
                    if h == 1 and opart[-1] == 7:
                        ygb_wfree = ("t", op_t)
                return f
            cls.append(c_outproj_a([0, 1, 2, 3]))
            cls.append(c_outproj_a([4, 5, 6, 7]))

            def c_mr():
                def f():
                    dep("V", "s", cur("s"))
                    run("V", lambda e: e.tensor_tensor(
                        out=tslice(x2, TC, t0, t1), in0=tslice(x2, TC, t0, t1),
                        in1=tslice(xnb, TC + 1, t0, t1), op=ALU.add), inc="v")
                return f
            cls.append(vh(c_mr()))

            ln_bc = [0]

            def c_stats(src_fn, extra=()):
                def f():
                    emit_ln(src_fn, n, stat_cast,
                            deps=list(extra) + [("v", cur("v"))])
                return f

            def c_mixapply():
                def f():
                    ln_bc[0] = emit_ln_fin()
                    dep("V", "t", ln_bc[0])
                    ln_apply_to_G(lambda: tslice(x2, TC, t0, t1), n, t0, t1)
                    dep("V", "s", G_s[0])
                    vv = None
                    for q in range(QD):
                        vv = run("V", lambda e, q=q: e.scalar_tensor_tensor(
                            out=x2[:, q * TC + t0:q * TC + t1],
                            in0=G[:, q * (TC + 1) + t0:q * (TC + 1) + t1],
                            scalar=cv("lng", q),
                            in1=cv("lnb", q).broadcast_to([128, n]),
                            op0=ALU.mult, op1=ALU.add), inc="v")
                    G_v[0] = vv
                    bank_consumed(5, "v", vv)
                    bank_consumed(4, "v", vv)
                return f

            def c_ln2apply():
                def f():
                    nonlocal X_free_h
                    ln_bc[0] = emit_ln_fin()
                    dep("V", "t", ln_bc[0])
                    ln_apply_to_G(lambda: tslice(x2, TC, t0, t1), n, t0, t1)
                    vv = run("V", lambda e: e.tensor_tensor(
                        out=tslice(x2, TC, t0, t1), in0=tslice(G, TC + 1, t0, t1),
                        in1=tslice(X, TC + 1, t0, t1), op=ALU.add), inc="v")
                    G_v[0] = vv
                    X_free_h[h] = ("v", vv)
                    bank_consumed(5, "v", vv)
                    bank_consumed(4, "v", vv)
                return f

            def c_ln3apply():
                def f():
                    nonlocal x2b_free_h
                    ln_bc[0] = emit_ln_fin()
                    cn, vv0 = fb_wfree
                    dep("V", cn, vv0)
                    dep("V", "t", ln_bc[0])
                    vv = ln_apply_to_G(lambda: tslice(x2, TC, t0, t1), n, t0, t1,
                                       out2=lambda: tslice(fb, TC, t0, t1))
                    bank_consumed(5, "v", vv)
                    bank_consumed(4, "v", vv)
                    clsn = "qci0" if c % 2 == 0 else "qci1"
                    dep("Q", clsn, cur(clsn))
                    dep("Q", "v", vv)
                    dep("Q", "s", cur("s"))
                    sv = run("Q", lambda e, c=c, t0=t0, t1=t1: e.dma_start(
                        out=x2scr[c % 2].rearrange("p (q j) -> p q j", j=TC)
                        [:, :, t0:t1],
                        in_=tslice(x2, TC, t0, t1)), inc=clsn, inck=16)
                    x2b_free_h[h] = (clsn, sv)
                    x2_reload_map.setdefault(c, []).append((clsn, sv))
                return f

            cls.append(c_stats(lambda: tslice(x2, TC, t0, t1)))
            cls.append(vh(c_mixapply()))
            cls.append(c_stats(lambda: tslice(x2, TC, t0, t1)))
            cls.append(vh(c_ln2apply()))
            cls.append(c_stats(lambda: tslice(x2, TC, t0, t1)))
            cls.append(vh(c_ln3apply()))
            return cls

        def pre_half_closures(cn, h):
            t0, t1 = (0, HB) if h == 0 else (HB, TC + 1)
            n = t1 - t0
            last = (cn == NCH - 1)
            cv0, cv1 = (0, 133) if h == 0 else (133, TC)
            cls = []

            ln_bc1 = [0]

            def c_xload():
                def f():
                    cnf, vvf = X_free_h[h]
                    dep("Q", cnf, vvf)
                    qcl = "qx"
                    dep("Q", qcl, cur(qcl))
                    tload = t1 if not (last and h == 1) else TC
                    xl = run("Q", lambda e, cn=cn, t0=t0, tload=tload: e.dma_start(
                        out=tslice(X, TC + 1, t0, tload),
                        in_=xT.rearrange("(q p) t -> p q t", p=128)
                        [:, :, cn * TC + t0:cn * TC + tload]), inc=qcl, inck=16)
                    if last and h == 1:
                        dep("V", qcl, xl)
                        xl2 = run("V", lambda e: e.memset(
                            tslice(X, TC + 1, TC, TC + 1), 0.0), inc="v")
                        emit_ln(lambda: tslice(X, TC + 1, t0, t1), n,
                                stat_cast, deps=[("v", xl2)])
                    else:
                        emit_ln(lambda: tslice(X, TC + 1, t0, t1), n,
                                stat_cast, deps=[(qcl, xl)])
                return f
            cls.append(c_xload())

            def c_ln1apply():
                def f():
                    nonlocal xnb_free_h
                    ln_bc1[0] = emit_ln_fin()
                    cnf, vvf = xnb_free_h[h]
                    dep("V", cnf, vvf)
                    dep("V", "t", ln_bc1[0])
                    vv = ln_apply_to_G(lambda: tslice(X, TC + 1, t0, t1), n, t0, t1,
                                       out2=lambda: tslice(xnb, TC + 1, t0, t1))
                    bank_consumed(5, "v", vv)
                    bank_consumed(4, "v", vv)
                    xnb_v[h] = vv
                return f
            cls.append(vh(c_ln1apply()))

            ipev = [0] * 8
            sp_sl = [0]
            xnb_v = [0, 0]

            def c_inproj(opart):
                def f():
                    s_x = cur("s")
                    v_x = xnb_v[h]
                    for o in opart:
                        so = o % 2
                        ldv = load_slab(slabm, slabm_free, so,
                                        wbf["ipT"].rearrange("(k p) j -> p k j", p=128)
                                        [:, :, o * 128:(o + 1) * 128],
                                        deps=cast_done_ld["ipT"])
                        bi = o % 2
                        pairs = [((lambda so=so, k=k: slabm[so][:, k * 128:(k + 1) * 128]),
                                  (lambda k=k, t0=t0, t1=t1:
                                   xnb[:, k * (TC + 1) + t0:k * (TC + 1) + t1]))
                                 for k in range(QD)]
                        mmdeps = [ldv, ("s", s_x), ("v", v_x)]
                        if o == opart[0]:
                            mmdeps.append(xc_free if o < 8 else zb_free)
                        tv = mm_group(bi, pairs, pb[bi][:, 0:n], deps=mmdeps)
                        slabm_free[so] = ("t", tv)
                        dep("S", "t", tv)
                        if o < 8:
                            cnf, vvf = xc_free
                            dep("S", cnf, vvf)
                            sv = run("S", lambda e, o=o, bi=bi, t0=t0, n=n:
                                     e.activation(
                                out=xc[:, o * (TC + 3) + 2 + t0:
                                       o * (TC + 3) + 2 + t0 + n],
                                in_=pb[bi][:, 0:n], func=AF.Identity,
                                bias=ipb_cols[o]), inc="s")
                            ipev[o] = sv
                        else:
                            cnf, vvf = zb_free
                            dep("S", cnf, vvf)
                            nz = min(t1, TC) - t0
                            sv = run("S", lambda e, o=o, bi=bi, t0=t0, nz=nz:
                                     e.activation(
                                out=zb[:, (o - 8) * TC + t0:(o - 8) * TC + t0 + nz],
                                in_=pb[bi][:, 0:nz], func=AF.Identity,
                                bias=ipb_cols[o]), inc="s")
                        bank_consumed(bi, "s", sv)
                    if opart[-1] == 15:
                        xnb_free_h[h] = ("t", cur("t"))
                return f
            cls.append(c_inproj(list(range(0, 8))))
            cls.append(c_inproj(list(range(8, 16))))

            def c_conv():
                def f():
                    nonlocal xc_free
                    if h == 0:
                        cnf, vvf = xc_free
                        dep("S", cnf, vvf)
                        sh = run("S", lambda e: e.copy(
                            out=xc[:].rearrange("p (q j) -> p q j", j=TC + 3)[:, :, 0:2],
                            in_=halo[:].rearrange("p (q j) -> p q j", j=2)), inc="s")
                        dep("V", "s", sh)
                    if last and h == 1:
                        dep("V", "s", ipev[7])
                        run("V", lambda e: e.memset(
                            xc[:].rearrange("p (q j) -> p q j", j=TC + 3)
                            [:, :, TC + 2:TC + 3], 0.0), inc="v")
                    nn = cv1 - cv0
                    dep("V", "s", G_s[0])
                    for q in range(QD):
                        dep("V", "s", ipev[q])
                        base = q * (TC + 3)
                        for k in range(4):
                            src = xc[:, base + cv0 + k:base + cv0 + k + nn]
                            wcol = cv("convw", q * 4 + k)
                            if k == 0:
                                run("V", lambda e, src=src, wcol=wcol, q=q, nn=nn:
                                    e.scalar_tensor_tensor(
                                        out=sa_s[:, 0:nn], in0=src, scalar=wcol,
                                        in1=cv("convb", q).broadcast_to([128, nn]),
                                        op0=ALU.mult, op1=ALU.add))
                            elif k < 3:
                                srcacc = sa_s[:, 0:nn] if k == 1 else raw_s[:, 0:nn]
                                dst = raw_s[:, 0:nn] if k == 1 else sa_s[:, 0:nn]
                                run("V", lambda e, src=src, wcol=wcol, i1=srcacc, d=dst:
                                    e.scalar_tensor_tensor(out=d, in0=src, scalar=wcol,
                                                           in1=i1, op0=ALU.mult,
                                                           op1=ALU.add))
                            else:
                                run("V", lambda e, src=src, wcol=wcol, q=q, nn=nn:
                                    e.scalar_tensor_tensor(
                                        out=G[:, q * (TC + 1) + cv0:
                                              q * (TC + 1) + cv0 + nn],
                                        in0=src, scalar=wcol, in1=sa_s[:, 0:nn],
                                        op0=ALU.mult, op1=ALU.add))
                    if h == 1:
                        vv = run("V", lambda e: e.tensor_copy(
                            out=halo[:].rearrange("p (q j) -> p q j", j=2),
                            in_=xc[:].rearrange("p (q j) -> p q j", j=TC + 3)
                            [:, :, TC:TC + 2]), inc="v")
                        xc_free = ("v", vv)
                        G_v[0] = vv
                    else:
                        vv = run("V", lambda e: e.engine_nop(), inc="v")
                        G_v[0] = vv
                    dep("S", "v", vv)
                    s_xs = run("S", lambda e, cv0=cv0, nn=nn: e.activation(
                        out=tslice(xs, TC, cv0, cv0 + nn),
                        in_=tslice(G[:, 0:QD * (TC + 1)], TC + 1, cv0, cv0 + nn),
                        func=AF.Silu), inc="s")
                    G_s[0] = s_xs
                    z1 = min(t1, TC)
                    s_sz = run("S", lambda e, t0=t0, z1=z1: e.activation(
                        out=tslice(szh, TC, t0, z1), in_=tslice(zb, TC, t0, z1),
                        func=AF.Silu), inc="s")
                    szh_map[(cn, h)] = s_sz
                return f
            cls.append(vh(c_conv()))

            def c_bc():
                def f():
                    nn = cv1 - cv0
                    s_c = run("S", lambda e, cv0=cv0, nn=nn: e.copy(
                        out=tslice(xsc, TC, cv0, cv0 + nn),
                        in_=tslice(xs, TC, cv0, cv0 + nn)), inc="s")
                    blk = h
                    fc, fv = bank[6]["free"]
                    dep("T", fc, fv)
                    dep("T", "s", s_c)
                    dep("T", "qmisc", bcldv)
                    tv = None
                    for k in range(QD):
                        a, bl = (k == 0), (k == QD - 1)
                        fn = (lambda e, blk=blk, k=k, a=a, bl=bl:
                              e.matmul(out=pb[6][:, 0:64],
                                       lhsT=xsc[:, k * TC + blk * 128:
                                                k * TC + blk * 128 + 128],
                                       rhs=bcw[:, k * 64:(k + 1) * 64],
                                       start=a, stop=bl))
                        tv = run("T", fn, inc="t") if bl else run("T", fn)
                    bank[6]["fill"] = ("t", tv)
                    dep("V", "t", tv)
                    v_bc = run("V", lambda e: e.tensor_tensor(
                        out=bcsb[:].rearrange("p (a b) -> p a b", b=64)[:, 0:1, :],
                        in0=pb[6][:, 0:64].unsqueeze(1),
                        in1=cv("bcb").unsqueeze(1).broadcast_to([128, 1, 64]),
                        op=ALU.add), inc="v")
                    bank_consumed(6, "v", v_bc)
                    dep("Q", "v", v_bc)
                    qbt_cls = "qbt0" if cn % 2 == 0 else "qbt1"
                    dep("Q", qbt_cls, cur(qbt_cls))
                    q_bt = run("Q", lambda e, cn=cn, blk=blk: e.dma_start(
                        out=btct[cn % 2].rearrange("(b p j) -> p b j", p=128, j=64)
                        [:, blk:blk + 1, :],
                        in_=bcsb[:].rearrange("p (a b) -> p a b", b=64)[:, 0:1, :]),
                        inc=qbt_cls, inck=16)
                    btct_map[(cn, h)] = (qbt_cls, q_bt)
                return f
            cls.append(c_bc())

            def c_dt():
                def f():
                    b0 = h * 128
                    for o in range(QD):
                        so = o % 2
                        ldv = load_slab(slabm, slabm_free, so,
                                        wbf["dtT"].rearrange("(k p) j -> p k j", p=128)
                                        [:, :, o * 128:(o + 1) * 128],
                                        deps=cast_done_ld["dtT"])
                        bi = o % 2
                        pairs = [((lambda so=so, k=k: slabm[so][:, k * 128:(k + 1) * 128]),
                                  (lambda k=k, b0=b0: xnb[:, k * (TC + 1) + b0:
                                                          k * (TC + 1) + b0 + 128]))
                                 for k in range(QD)]
                        tv = mm_group(bi, pairs, pb[bi][:, 0:128], deps=[ldv])
                        slabm_free[so] = ("t", tv)
                        dep("S", "t", tv)
                        sv = run("S", lambda e, o=o, bi=bi, b0=b0: e.activation(
                            out=zb[:, o * TC + b0:o * TC + b0 + 128],
                            in_=pb[bi][:, 0:128], func=AF.Identity,
                            bias=cv("dtbias", o)), inc="s")
                        bank_consumed(bi, "s", sv)
                    if h == 1:
                        xnb_free_h[0] = ("t", cur("t"))
                return f
            cls.append(c_dt())

            def c_softplus():
                def f():
                    nonlocal zb_free
                    b0 = h * 128
                    cn_, vv_ = out_dma_prev
                    dep("S", cn_, vv_)
                    dep("S", "v", G_v[0])
                    s_ab = run("S", lambda e, b0=b0: e.activation(
                        out=tslice(G[:, 0:QD * (TC + 1)], TC + 1, b0, b0 + 128),
                        in_=tslice(zb, TC, b0, b0 + 128), func=AF.Abs), inc="s")
                    dep("S", "s", s_ab)
                    s_ex = run("S", lambda e, b0=b0: e.activation(
                        out=dxb_t[:, b0 * QD:(b0 + 128) * QD],
                        in_=tslice(G[:, 0:QD * (TC + 1)], TC + 1, b0, b0 + 128),
                        func=AF.Exp, scale=-1.0), inc="s")
                    dep("S", "s", s_ex)
                    s_l = run("S", lambda e, b0=b0: e.activation(
                        out=tslice(G[:, 0:QD * (TC + 1)], TC + 1, b0, b0 + 128),
                        in_=dxb_t[:, b0 * QD:(b0 + 128) * QD], func=AF.Ln,
                        bias=c_one[:]), inc="s")
                    G_s[0] = s_l
                    sp_sl[0] = s_l
                return f

            def c_softplus_v():
                def f():
                    nonlocal zb_free
                    b0 = h * 128
                    s_l = sp_sl[0]
                    dep("V", "s", s_l)
                    run("V", lambda e, b0=b0: e.scalar_tensor_tensor(
                        out=tslice(G[:, 0:QD * (TC + 1)], TC + 1, b0, b0 + 128),
                        in0=tslice(zb, TC, b0, b0 + 128), scalar=0.0, op0=ALU.max,
                        in1=tslice(G[:, 0:QD * (TC + 1)], TC + 1, b0, b0 + 128),
                        op1=ALU.add))
                    vz = run("V", lambda e, b0=b0: e.tensor_scalar(
                        out=dtb_t[:, b0 * QD:(b0 + 128) * QD].rearrange(
                            "p (t q) -> p t q", q=QD).rearrange("p t q -> p q t"),
                        in0=tslice(G[:, 0:QD * (TC + 1)], TC + 1, b0, b0 + 128),
                        scalar1=0.2, scalar2=1e-3, op0=ALU.min, op1=ALU.max), inc="v")
                    zb_free = ("v", vz)
                    vd = run("V", lambda e, b0=b0: e.tensor_tensor(
                        out=dxb_t[:, b0 * QD:(b0 + 128) * QD].rearrange(
                            "p (t q) -> p t q", q=QD).rearrange("p t q -> p q t"),
                        in0=dtb_t[:, b0 * QD:(b0 + 128) * QD].rearrange(
                            "p (t q) -> p t q", q=QD).rearrange("p t q -> p q t"),
                        in1=tslice(xs, TC, b0, b0 + 128), op=ALU.mult), inc="v")
                    G_v[0] = vd
                    vdt_map[(cn, h)] = vd
                return f
            cls.append(c_softplus())
            cls.append(vh(c_softplus_v()))
            return cls

        for c in range(NCH + 1):
            cm1 = c - 1
            have_cur = c < NCH
            have_prev = cm1 >= 0

            cell = {"g1_s": 0, "g2_s": 0, "h1_t": 0, "h2_t": 0}
            if have_prev:
                ffn_tasks = mk_ffn_tasks(cm1, cell)
                ffn_ok[0] = False
            else:
                ffn_tasks = cast_tasks
                ffn_ok[0] = True

            if c == 0:
                for f in pre_half_closures(0, 0):
                    f()
                for f in pre_half_closures(0, 1):
                    f()

            if have_cur:
                g0_chunk = gw[0]

                def emit_am(w, c=c, g0_chunk=g0_chunk):
                    g = g0_chunk + w
                    b3 = g % 3
                    cnf, vvf = win_vfree.get(g - 3, ("v", 0))
                    dep("P", cnf, vvf)
                    dep("P", "v", vdt_map[(c, w // 16)])
                    pa = run("P", lambda e, w=w, b3=b3: e.tensor_tensor(
                        out=amb[b3][:].rearrange("p (t q n) -> p t q n", q=QD, n=NST),
                        in0=dtb_t[:, w * W * QD:(w + 1) * W * QD].rearrange(
                            "p (t q) -> p t q", q=QD).unsqueeze(3).broadcast_to(
                            [128, W, QD, NST]),
                        in1=Ap[:].rearrange("p (q n) -> p q n", n=NST).unsqueeze(1)
                        .broadcast_to([128, W, QD, NST]), op=ALU.mult), inc="p")
                    dep("S", cnf, vvf)
                    dep("S", "p", pa)
                    se = run("S", lambda e, b3=b3: e.activation(
                        out=abar[b3][:], in_=amb[b3][:], func=AF.Exp, scale=-1.0),
                        inc="s")
                    return se

                def emit_u(w, c=c, g0_chunk=g0_chunk):
                    g = g0_chunk + w
                    b2 = g % 2
                    cnf, vvf = win_vfree.get(g - 2, ("v", 0))
                    bcls, bval = btct_map[(c, w // 16)]
                    dep("S", cnf, vvf)
                    dep("S", bcls, bval)
                    sbr = ("sbr0", "sbr1")[b2]
                    scr = ("scr0", "scr1")[b2]
                    db = run("S", lambda e, w=w, b2=b2, c=c: e.dma_start(
                        out=brep[b2][:].rearrange("p (t j) -> p t j", j=NST),
                        in_=btct[c % 2].rearrange("(t j) -> t j", j=64)
                        [w * W:(w + 1) * W, 0:NST].unsqueeze(0)
                        .broadcast_to([128, W, NST])), inc=sbr, inck=16)
                    dc = run("S", lambda e, w=w, b2=b2, c=c: e.dma_start(
                        out=crep[b2][:].rearrange("p (t j) -> p t j", j=NST),
                        in_=btct[c % 2].rearrange("(t j) -> t j", j=64)
                        [w * W:(w + 1) * W, NST:64].unsqueeze(0)
                        .broadcast_to([128, W, NST])), inc=scr, inck=16)
                    dep("P", cnf, vvf)
                    dep("P", "v", vdt_map[(c, w // 16)])
                    dep("P", sbr, db)
                    pu = run("P", lambda e, w=w, b2=b2: e.tensor_tensor(
                        out=ub[b2][:].rearrange("p (t q n) -> p t q n", q=QD, n=NST),
                        in0=dxb_t[:, w * W * QD:(w + 1) * W * QD].rearrange(
                            "p (t q) -> p t q", q=QD).unsqueeze(3).broadcast_to(
                            [128, W, QD, NST]),
                        in1=brep[b2][:].rearrange("p (t n) -> p t n", n=NST)
                        .unsqueeze(2).broadcast_to([128, W, QD, NST]),
                        op=ALU.mult), inc="p")
                    return pu, dc, scr

                amq = [emit_am(0), emit_am(1)]
                uq = [emit_u(0), emit_u(1)]
                for w in range(NWIN):
                    g = gw[0]
                    b3 = g % 3
                    b2 = g % 2
                    se = amq.pop(0)
                    pu, dc, scr = uq.pop(0)
                    dep("V", "s", se)
                    dep("V", "p", pu)
                    v_mid = None
                    for tr in range(W):
                        prev = (tr - 1) % W
                        run("V", lambda e, prev=prev, tr=tr, b3=b3: e.tensor_tensor(
                            out=sa_s[:], in0=chist[:, prev * DN:(prev + 1) * DN],
                            in1=abar[b3][:, tr * DN:(tr + 1) * DN], op=ALU.mult))
                        run("V", lambda e, tr=tr, b2=b2: e.tensor_tensor(
                            out=raw_s[:], in0=sa_s[:],
                            in1=ub[b2][:, tr * DN:(tr + 1) * DN], op=ALU.add))
                        v_mid = run("V", lambda e, tr=tr: e.scalar_tensor_tensor(
                            out=chist[:, tr * DN:(tr + 1) * DN], in0=raw_s[:],
                            scalar=10.0, in1=ncap10[:].broadcast_to([128, DN]),
                            op0=ALU.min, op1=ALU.max),
                            inc="v" if tr == W - 1 else None)
                    dep("V", scr, dc)
                    run("V", lambda e, b3=b3, b2=b2: e.tensor_tensor(
                        out=amb[b3][:].rearrange("p (t q n) -> p t q n", q=QD, n=NST),
                        in0=chist[:].rearrange("p (t q n) -> p t q n", q=QD, n=NST),
                        in1=crep[b2][:].rearrange("p (t n) -> p t n", n=NST)
                        .unsqueeze(2).broadcast_to([128, W, QD, NST]),
                        op=ALU.mult))
                    run("V", lambda e, b3=b3: e.tensor_reduce(
                        out=ysw[:], in_=amb[b3][:].rearrange("p (tq n) -> p tq n",
                                                             n=NST),
                        axis=AX.X, op=ALU.add))
                    hw = 0 if w < 17 else 1
                    dep("V", "s", szh_map[(c, hw)])
                    if w == 17:
                        cnf, vvf = ygb_wfree
                        dep("V", cnf, vvf)
                    vr = run("V", lambda e, w=w: e.tensor_tensor(
                        out=xsb[:].rearrange("p (q j) -> p q j", j=TC)
                        [:, :, w * W:(w + 1) * W].rearrange("p q j -> p j q"),
                        in0=ysw[:].rearrange("p (t q) -> p t q", q=QD),
                        in1=szh[:].rearrange("p (q j) -> p q j", j=TC)
                        [:, :, w * W:(w + 1) * W].rearrange("p q j -> p j q"),
                        op=ALU.mult), inc="v")
                    win_vfree[g] = ("v", vr)
                    if w == 16:
                        gate_map[(c, 0)] = vr
                    if w == NWIN - 1:
                        gate_map[(c, 1)] = vr
                    if w + 2 < NWIN:
                        amq.append(emit_am(w + 2))
                        uq.append(emit_u(w + 2))
                    gw[0] += 1
                    if w == 17:
                        postA = post_half_closures(c, 0)
                        preA = (pre_half_closures(c + 1, 0)
                                if c + 1 < NCH else [None] * 9)
                        order = [postA[0], postA[1], postA[2], postA[3],
                                 postA[4], postA[5], postA[6], preA[0],
                                 preA[1], preA[2], preA[3], postA[7],
                                 postA[8], preA[4], preA[5], preA[6],
                                 preA[7], preA[8]]
                        for f_ in order:
                            if f_ is not None:
                                side.append(f_)
                    for _ in range(2):
                        if side:
                            side.popleft()()
                    nff = 2 if c == 0 else FFN_INTERLEAVE
                    if ffn_ok[0]:
                        for _ in range(nff):
                            if ffn_tasks:
                                ffn_tasks.popleft()()
                while ffn_tasks:
                    ffn_tasks.popleft()()

            def mk_ln4(cm1):
                bc4c = [0]

                def f_stats():
                    s_fdone = cur("s")
                    emit_ln(lambda: qv(Fb[:], TC), TC, stat_cast,
                            deps=[("t", cur("t")), ("s", s_fdone)])
                    bc4c.append(s_fdone)

                def f_apply():
                    nonlocal out_dma_prev
                    bc4c[0] = emit_ln_fin()
                    dep("V", "s", bc4c[1])
                    dep("V", "s", G_s[0])
                    dep("V", "t", bc4c[0])
                    rstd4 = pb[5][:, 0:TC].unsqueeze(1).broadcast_to([128, QD, TC])
                    p4 = pb[4][:, 0:TC].unsqueeze(1).broadcast_to([128, QD, TC])
                    run("V", lambda e, r=rstd4: e.tensor_tensor(
                        out=qv(G[:, 0:QD * TC], TC), in0=qv(Fb[:], TC), in1=r,
                        op=ALU.mult))
                    v_t1 = run("V", lambda e, p_=p4: e.tensor_tensor(
                        out=qv(G[:, 0:QD * TC], TC), in0=qv(G[:, 0:QD * TC], TC),
                        in1=p_, op=ALU.add), inc="v")
                    G_v[0] = v_t1
                    bank_consumed(5, "v", v_t1)
                    bank_consumed(4, "v", v_t1)
                    for cls_, vv_ in x2_reload_map.get(cm1, []):
                        dep("Q", cls_, vv_)
                    dep("Q", "v", v_t1)
                    rcl = "qci0" if cm1 % 2 == 0 else "qci1"
                    dep("Q", rcl, cur(rcl))
                    rl = run("Q", lambda e, cm1=cm1: e.dma_start(
                        out=Fb[:], in_=x2scr[cm1 % 2]), inc=rcl, inck=16)
                    dep("V", rcl, rl)
                    v_fin = run("V", lambda e: e.tensor_tensor(
                        out=Fb[:], in0=G[:, 0:QD * TC], in1=Fb[:], op=ALU.add),
                        inc="v")
                    G_v[0] = v_fin
                    dep("Q", "v", v_fin)
                    sv = run("Q", lambda e, cm1=cm1: e.dma_start(
                        out=oT.rearrange("(q p) t -> p q t", p=128)
                        [:, :, cm1 * TC:(cm1 + 1) * TC], in_=qv(Fb[:], TC)),
                        inc="st", inck=16)
                    out_dma_prev = ("st", sv)
                return [f_stats, vh(f_apply)]

            if not have_cur:
                if have_prev:
                    while side:
                        side.popleft()()
                    while ffn_tasks:
                        ffn_tasks.popleft()()
                    for f_ in mk_ln4(cm1):
                        f_()
                break

            postB = post_half_closures(c, 1)
            if have_prev:
                ln4s, ln4a = mk_ln4(cm1)
                side.append(postB.pop(0))
                side.append(ln4s)
                side.append(postB.pop(0))
                side.append(ln4a)
            for f_ in postB:
                side.append(f_)
            w1w2s = [w1w2_group(c, list(range(i, i + 8))) for i in range(0, 32, 8)]
            preB = (pre_half_closures(c + 1, 1) if c + 1 < NCH else [None] * 9)
            orderB = [w1w2s[0], w1w2s[1], preB[0], preB[1], w1w2s[2], preB[2],
                      preB[3], w1w2s[3], preB[4], preB[5], preB[6], preB[7],
                      preB[8]]
            for f_ in orderB:
                if f_ is not None:
                    side.append(f_)

        dep("Q", "st", cur("st"))

        @block.sync
        def _(sync):
            for fn in prog["Q"]:
                fn(sync)

        @block.vector
        def _(vector):
            for fn in prog["V"]:
                fn(vector)

        @block.scalar
        def _(scalar):
            for fn in prog["S"]:
                fn(scalar)

        @block.tensor
        def _(tensor):
            for fn in prog["T"]:
                fn(tensor)

        @block.gpsimd
        def _(gpsimd):
            for fn in prog["P"]:
                fn(gpsimd)

    return nc


def _prep_consts(inp):
    cols = np.zeros((128, CK), dtype=np.float32)

    def put(name, arr):
        o = _offs[name]
        cols[:, o:o + arr.shape[1]] = arr
    put("ipb", np.ascontiguousarray(inp["L0_attn_in_proj_b"].reshape(16, 128).T))
    cw = inp["L0_attn_conv_w"][:, 0, :]
    put("convw", np.ascontiguousarray(
        cw.reshape(8, 128, 4).transpose(1, 0, 2).reshape(128, 32)))
    put("convb", np.ascontiguousarray(inp["L0_attn_conv_b"].reshape(8, 128).T))
    put("dtbias", np.ascontiguousarray(inp["L0_attn_dt_b"].reshape(8, 128).T))
    put("outb", np.ascontiguousarray(inp["L0_attn_out_b"].reshape(8, 128).T))
    put("lng", np.ascontiguousarray(inp["L0_attn_ln_g"].reshape(8, 128).T))
    put("lnb", np.ascontiguousarray(inp["L0_attn_ln_beta"].reshape(8, 128).T))
    al = inp["L0_attn_A_log"]
    put("alog", np.ascontiguousarray(
        al.reshape(8, 128, 32).transpose(1, 0, 2).reshape(128, 256)))
    put("w1s", np.ascontiguousarray(inp["L0_ffn_w1_s"].reshape(32, 128).T))
    put("w2s", np.ascontiguousarray(inp["L0_ffn_w2_s"].reshape(32, 128).T))
    put("h1s", np.ascontiguousarray(inp["L0_ffn_h1_s"].reshape(32, 128).T))
    put("h2s", np.ascontiguousarray(inp["L0_ffn_h2_s"].reshape(32, 128).T))
    put("w3s", np.ascontiguousarray(inp["L0_ffn_w3_s"].reshape(8, 128).T))
    bcb = np.concatenate([inp["L0_attn_B_b"], inp["L0_attn_C_b"]])
    put("bcb", np.tile(bcb[None, :], (128, 1)))
    return cols


def kernel(**inp):
    inp = {k: np.asarray(v) for k, v in inp.items()}
    x = inp["L0_attn_x"]
    consts = _prep_consts(inp)
    wT = {
        "ipT": np.ascontiguousarray(inp["L0_attn_in_proj_w"].T),
        "dtT": np.ascontiguousarray(inp["L0_attn_dt_w"].T),
        "outT": np.ascontiguousarray(inp["L0_attn_out_w"].T),
        "bcT": np.ascontiguousarray(
            np.concatenate([inp["L0_attn_B_w"], inp["L0_attn_C_w"]], axis=0).T),
        "w1T": np.ascontiguousarray(inp["L0_ffn_w1_w"].T),
        "w2T": np.ascontiguousarray(inp["L0_ffn_w2_w"].T),
        "h1T": np.ascontiguousarray(inp["L0_ffn_h1_w"].T),
        "h2T": np.ascontiguousarray(inp["L0_ffn_h2_w"].T),
        "w3T": np.ascontiguousarray(inp["L0_ffn_w3_w"].T),
    }
    nc = build_program()
    in_maps = []
    for i in range(8):
        m = {"xT": np.ascontiguousarray(x[i].T), "consts": consts}
        m.update(wT)
        in_maps.append(m)
    res = run_bass_kernel_spmd(nc, in_maps, core_ids=list(range(8)))
    out = np.stack([np.ascontiguousarray(r["oT"].T) for r in res.results], axis=0)
    return out


# version 24
# speedup vs baseline: 1.0144x; 1.0144x over previous
import os
os.environ.setdefault("GAUGE_NTFF_IGNORE_DMA", "1")

import numpy as np
from contextlib import ExitStack
from collections import deque

import concourse.bass as bass
import concourse.mybir as mybir
from concourse.bass_utils import run_bass_kernel_spmd

F32 = mybir.dt.float32
BF16 = mybir.dt.bfloat16
FP16 = mybir.dt.float16
ALU = mybir.AluOpType
AF = mybir.ActivationFunctionType
AX = mybir.AxisListType

D = 1024
T = 4096
TC = 256
NCH = T // TC
W = 8
NWIN = TC // W
QD = 8
NST = 32
DN = QD * NST
EPS = 1e-5

_offs = {}
_ck = 0
_cw = {"ipb": 16, "convw": 32, "convb": 8, "dtbias": 8, "outb": 8, "lng": 8,
       "lnb": 8, "alog": 256, "w1s": 32, "w2s": 32, "h1s": 32, "h2s": 32,
       "w3s": 8, "bcb": 64}
for _n, _w in _cw.items():
    _offs[_n] = _ck
    _ck += _w
CK = _ck

DEBUG_TAPS = {}
FFN_INTERLEAVE = 3


def build_program():
    nc = bass.Bass("TRN2", target_bir_lowering=False)

    xT = nc.dram_tensor("xT", [D, T], F32, kind="ExternalInput").ap()
    consts_d = nc.dram_tensor("consts", [128, CK], F32, kind="ExternalInput").ap()
    win = {}
    for nm, shp in [("ipT", [D, 2 * D]), ("dtT", [D, D]), ("outT", [D, D]),
                    ("bcT", [D, 64]), ("w1T", [D, 4 * D]), ("w2T", [D, 4 * D]),
                    ("h1T", [4 * D, 4 * D]), ("h2T", [4 * D, 4 * D]),
                    ("w3T", [4 * D, D])]:
        win[nm] = nc.dram_tensor(nm, shp, F32, kind="ExternalInput").ap()
    wbf = {nm: nc.dram_tensor(nm + "b", list(win[nm].shape), BF16, kind="Internal").ap()
           for nm in win}
    oT = nc.dram_tensor("oT", [D, T], F32, kind="ExternalOutput").ap()
    dbg_t = {}
    for _dn in DEBUG_TAPS:
        dbg_t[_dn] = None
    btct = [nc.dram_tensor(f"btct{i}", [TC * 64], F32, kind="Internal").ap()
            for i in range(2)]
    statscr = [nc.dram_tensor(f"stsc{i}", [520], F32, kind="Internal").ap()
               for i in range(4)]
    lnmu_d = nc.dram_tensor("lnmu", [256], F32, kind="Internal").ap()
    lnrs_d = nc.dram_tensor("lnrs", [256], F32, kind="Internal").ap()

    es = ExitStack()
    with es:
        def sb(name, shape, dt=F32):
            return es.enter_context(nc.sbuf_tensor(name, shape, dt))

        consts = sb("consts_sb", [128, CK])
        Ap = sb("Ap_sb", [128, DN])
        ones_bf = sb("ones_bf", [128, 1], BF16)
        ones_row = sb("ones_row", [1, 128])
        cap10 = sb("cap10", [128, 1])
        ncap10 = sb("ncap10", [128, 1])
        c_eps = sb("c_eps", [128, 1])
        c_one = sb("c_one", [128, 1])
        X = sb("X", [128, QD * (TC + 1)])
        xnb = sb("xnb", [128, QD * (TC + 1)], BF16)
        xc = sb("xc", [128, QD * (TC + 3)])
        halo = sb("halo", [128, QD * 2])
        zb = sb("zb", [128, QD * TC])
        szh = sb("szh", [128, QD * TC], FP16)
        xs = sb("xs", [128, QD * TC], FP16)
        xsb = sb("xsb", [128, QD * TC], BF16)
        xsc = sb("xsc", [128, QD * TC], BF16)
        sw = sb("sw", [128, 8 * TC], FP16)
        sa_s = sb("sa_s", [128, DN])
        raw_s = sb("raw_s", [128, DN])
        acc_s = sa_s
        dtb_t = sb("dtb_t", [128, TC * QD], FP16)
        dxb_t = sb("dxb_t", [128, TC * QD], FP16)
        chist = sb("chist", [128, W * DN])
        amb = [sb(f"amb{i}", [128, W * DN], FP16) for i in range(3)]
        abar = [sb(f"abar{i}", [128, W * DN]) for i in range(3)]
        ub = [sb(f"ub{i}", [128, W * DN], FP16) for i in range(2)]
        brep = [sb(f"brep{i}", [128, W * NST]) for i in range(2)]
        crep = [sb(f"crep{i}", [128, W * NST]) for i in range(2)]
        hb = sb("hb", [128, 32 * TC], BF16)
        g1b = sb("g1b", [128, 32 * TC], BF16)
        x2 = sb("x2", [128, QD * TC])
        fb = sb("fb", [128, QD * TC], BF16)
        Fb = sb("Fb", [128, QD * TC])
        G = sb("G", [128, QD * (TC + 1)])
        stat = sb("stat", [128, 784])
        ysw = sb("ysw", [128, W * QD])
        bcsb = sb("bcsb", [128, 64])
        stat_cast = sb("stat_cast", [128, QD * (TC + 1)], BF16)
        slabm = [sb(f"slabm{i}", [128, QD * 128], BF16) for i in range(2)]
        slabf = [sb(f"slabf{i}", [128, 16 * 128], BF16) for i in range(4)]
        bcw = sb("bcw", [128, QD * 64], BF16)

        pb = [es.enter_context(nc.psum_tensor(f"pb{i}", [128, 512], F32))
              for i in range(8)]

        sems = {}
        for snm in ["v", "p", "s", "t", "st", "dtap",
                    "qm0", "qm1", "qf0", "qf1", "qf2", "qf3", "qx", "qbt0", "qbt1",
                    "wc0", "wc1", "wc2", "wc3", "wc4", "wc5", "wc6", "wc7",
                    "lnb0", "lnb1",
                    "qci0", "qci1", "qco0", "qco1", "qmisc",
                    "sscr", "srep0", "srep1", "sbr0", "sbr1", "scr0", "scr1"]:
            sems[snm] = [es.enter_context(nc.semaphore(snm)), 0]

        block = es.enter_context(nc.Block())

        prog = {k: [] for k in "QVPST"}
        lastw = {}

        def dep(key, cname, val):
            if val is None or val <= 0:
                return
            if lastw.get((key, cname), 0) >= val:
                return
            lastw[(key, cname)] = val
            sem = sems[cname][0]
            prog[key].append(lambda e, s=sem, v=val: e.wait_ge(s, v))

        def run(key, fn, inc=None, inck=1):
            if inc is None:
                prog[key].append(fn)
                return None
            sems[inc][1] += inck
            v = sems[inc][1]
            sem = sems[inc][0]

            def g(e, fn=fn, s=sem, k=inck):
                fn(e).then_inc(s, k)
            prog[key].append(g)
            return v

        def cur(cname):
            return sems[cname][1]

        bank = [{"fill": ("t", 0), "free": ("v", 0)} for _ in range(8)]

        def bank_consumed(bi, cname, val):
            bank[bi]["free"] = (cname, val)

        def mm_group(bi, pairs, out_ap, deps=()):
            fc, fv = bank[bi]["free"]
            dep("T", fc, fv)
            for cn, v in deps:
                dep("T", cn, v)
            n = len(pairs)
            tv = None
            for i, (lf, rf) in enumerate(pairs):
                a, b = (i == 0), (i == n - 1)
                fn = (lambda e, lf=lf, rf=rf, o=out_ap, a=a, b=b:
                      e.matmul(out=o, lhsT=lf(), rhs=rf(), start=a, stop=b))
                if b:
                    tv = run("T", fn, inc="t")
                else:
                    run("T", fn)
            bank[bi]["fill"] = ("t", tv)
            return tv

        def tap(name, c_now, src_ap, dtype, deplist):
            if DEBUG_TAPS.get(name) != c_now:
                return
            shp = list(src_ap.shape)
            h = nc.dram_tensor("dbg_" + name, shp, dtype, kind="ExternalOutput").ap()
            for cn, v in deplist:
                dep("Q", cn, v)
            run("Q", lambda e, h=h, s=src_ap: e.dma_start(out=h, in_=s),
                inc="dtap", inck=16)
            for k in "VPSTQ":
                dep(k, "dtap", cur("dtap"))

        def cv(name, i=None):
            o = _offs[name]
            if i is None:
                return consts[:, o:o + _cw[name]]
            return consts[:, o + i:o + i + 1]

        def qv(t_ap, ncol):
            return t_ap.rearrange("p (q j) -> p q j", j=ncol)

        slabm_free = [("t", 0), ("t", 0)]
        slabf_free = [("t", 0)] * 4
        ffn_rr = [0]
        w12_pend = deque()
        sw_v = [0, 0, 0, 0]

        def load_slab(buf_list, free_list, slot, src_ap, deps=()):
            fc, fv = free_list[slot]
            dep("Q", fc, fv)
            for cn, v in deps:
                dep("Q", cn, v)
            cls = (("qm0", "qm1") if buf_list is slabm
                   else ("qf0", "qf1", "qf2", "qf3"))
            ncols = src_ap.size() // 128
            v = run("Q", lambda e, b=buf_list[slot], s=src_ap, nn=ncols:
                    e.dma_start(out=b[:, 0:nn], in_=s), inc=cls[slot], inck=16)
            return (cls[slot], v)

        run("Q", lambda e: e.dma_start(out=consts[:], in_=consts_d),
            inc="qmisc", inck=16)
        ldc = cur("qmisc")
        dep("V", "qmisc", ldc)
        run("V", lambda e: e.memset(ones_bf[:], 1.0))
        run("V", lambda e: e.memset(ones_row[:], 1.0))
        run("V", lambda e: e.memset(cap10[:], 10.0))
        run("V", lambda e: e.memset(ncap10[:], -10.0))
        run("V", lambda e: e.memset(c_eps[:], EPS))
        run("V", lambda e: e.memset(c_one[:], 1.0))
        run("V", lambda e: e.memset(chist[:, (W - 1) * DN:W * DN], 0.0))
        run("V", lambda e: e.memset(halo[:], 0.0))
        run("V", lambda e: e.tensor_scalar(out=G[:, 0:DN], in0=cv("alog"),
                                           scalar1=2.0, scalar2=-5.0,
                                           op0=ALU.min, op1=ALU.max))
        v_init = run("V", lambda e: e.engine_nop(), inc="v")
        dep("S", "v", v_init)
        s_ap = run("S", lambda e: e.activation(out=Ap[:], in_=G[:, 0:DN],
                                               func=AF.Exp), inc="s")
        dep("P", "s", s_ap)
        dep("T", "v", v_init)

        cast_dma = ["bcT", "ipT", "dtT", "outT", "w1T", "w2T", "w3T", "h2T"]
        cast_ffn = [("h1T", 4 * D, 4 * D, "w2s")]
        ci = [hb[:, 0:4096].bitcast(F32), hb[:, 4096:8192].bitcast(F32)]
        co = [g1b[:, 0:4096].bitcast(BF16), g1b[:, 4096:8192].bitcast(BF16)]
        cast_cp = [("v", 0), ("v", 0)]
        cast_out = [("qco0", 0), ("qco1", 0)]
        cast_pi = [0]
        cast_done_ld = {}

        def emit_cast_tile(nm, src, dst, r0, c0, cc, scnm, is_last):
            pi = cast_pi[0]
            b = pi % 2
            qci = "qci0" if b == 0 else "qci1"
            qco = "qco0" if b == 0 else "qco1"
            cn, vv0 = cast_cp[b]
            dep("Q", cn, vv0)
            ldv = run("Q", lambda e, s=src, r0=r0, c0=c0, cc=cc, b=b:
                      e.dma_start(out=ci[b][:, 0:cc],
                                  in_=s[r0:r0 + 128, c0:c0 + cc]),
                      inc=qci, inck=16)
            key = "S" if (scnm is not None or pi % 2 == 0) else "V"
            dep(key, qci, ldv)
            cn2, vv2 = cast_out[b]
            dep(key, cn2, vv2)
            if scnm is not None:
                kq = r0 // 128
                cpv = run("S", lambda e, b=b, cc=cc, col=cv(scnm, kq):
                          e.activation(out=co[b][:, 0:cc], in_=ci[b][:, 0:cc],
                                       func=AF.Identity, scale=col), inc="s")
                cpc = "s"
            elif key == "S":
                cpv = run("S", lambda e, b=b, cc=cc:
                          e.copy(out=co[b][:, 0:cc], in_=ci[b][:, 0:cc]),
                          inc="s")
                cpc = "s"
            else:
                cpv = run("V", lambda e, b=b, cc=cc:
                          e.tensor_copy(out=co[b][:, 0:cc], in_=ci[b][:, 0:cc]),
                          inc="v")
                cpc = "v"
            cast_cp[b] = (cpc, cpv)
            dep("Q", cpc, cpv)
            ldo = run("Q", lambda e, d=dst, r0=r0, c0=c0, cc=cc, b=b:
                      e.dma_start(out=d[r0:r0 + 128, c0:c0 + cc],
                                  in_=co[b][:, 0:cc]), inc=qco, inck=16)
            cast_out[b] = (qco, ldo)
            cast_pi[0] += 1
            if is_last:
                cast_done_ld[nm] = [("qco0", cur("qco0")), ("qco1", cur("qco1"))]

        def tiles_of(nm, R, C, scnm):
            src, dst = win[nm], wbf[nm]
            tl = [(r0, c0, min(2048, C - c0))
                  for r0 in range(0, R, 128) for c0 in range(0, C, 2048)]
            return [(nm, src, dst, r0, c0, cc, scnm, i == len(tl) - 1)
                    for i, (r0, c0, cc) in enumerate(tl)]

        for i, nm in enumerate(cast_dma):
            wcl = f"wc{i}"
            v = run("P", lambda e, nm=nm: e.dma_start(
                out=wbf[nm].rearrange("a b -> (a b)"),
                in_=win[nm].rearrange("a b -> (a b)")), inc=wcl, inck=16)
            cast_done_ld[nm] = [(wcl, v)]

        cast_tasks = deque()
        for nm, R, C, scnm in cast_ffn:
            for args in tiles_of(nm, R, C, scnm):
                cast_tasks.append(lambda a=args: emit_cast_tile(*a))

        for cn_, v_ in cast_done_ld["bcT"]:
            dep("Q", cn_, v_)
        dep("Q", "qmisc", ldc)
        bcldv = run("Q", lambda e: e.dma_start(
            out=bcw[:], in_=wbf["bcT"].rearrange("(q p) j -> p q j", p=128)),
            inc="qmisc", inck=16)
        bcld_cls = "qmisc" 

        ln_ring = [0]
        scr_ring = [0]
        last_d1 = [0]
        last_vp = [0]
        last_tp = [0]
        scr_free = [("v", 0)] * 4
        rep_free = [("v", 0)] * 2

        ln_state = {}

        def emit_ln(src_fn, ncol, cast_buf, deps=()):
            for cn, v in deps:
                dep("S", cn, v)
            c1 = run("S", lambda e: e.copy(
                out=cast_buf[:, 0:QD * ncol].rearrange("p (q j) -> p q j", j=ncol),
                in_=src_fn()), inc="s")
            fc, fv = bank[5]["free"]
            dep("T", fc, fv)
            dep("T", "s", c1)
            t1 = None
            for q in range(QD):
                a, bl = (q == 0), (q == QD - 1)
                fn = (lambda e, q=q, ncol=ncol, cb=cast_buf, a=a, bl=bl:
                      e.matmul(out=pb[5][0:1, 0:ncol], lhsT=ones_bf[:],
                               rhs=cb[:, q * ncol:(q + 1) * ncol], start=a, stop=bl))
                t1 = run("T", fn, inc="t") if bl else run("T", fn)
            bank[5]["fill"] = ("t", t1)
            dep("S", "t", t1)
            c2 = run("S", lambda e: e.activation(
                out=cast_buf[:, 0:QD * ncol].rearrange("p (q j) -> p q j", j=ncol),
                in_=src_fn(), func=AF.Square), inc="s")
            fc, fv = bank[4]["free"]
            dep("T", fc, fv)
            dep("T", "s", c2)
            t2 = None
            for q in range(QD):
                a, bl = (q == 0), (q == QD - 1)
                fn = (lambda e, q=q, ncol=ncol, cb=cast_buf, a=a, bl=bl:
                      e.matmul(out=pb[4][0:1, 0:ncol], lhsT=ones_bf[:],
                               rhs=cb[:, q * ncol:(q + 1) * ncol], start=a, stop=bl))
                t2 = run("T", fn, inc="t") if bl else run("T", fn)
            bank[4]["fill"] = ("t", t2)
            dep("S", "t", t1)
            dep("S", "t", last_tp[0])
            s_mu = run("S", lambda e, ncol=ncol: e.activation(
                out=stat[0:1, 0:ncol], in_=pb[5][0:1, 0:ncol],
                func=AF.Identity, scale=1.0 / D), inc="s")
            bank_consumed(5, "s", s_mu)
            dep("S", "s", s_mu)
            s_m2 = run("S", lambda e, ncol=ncol: e.activation(
                out=stat[32:33, 0:ncol], in_=stat[0:1, 0:ncol], func=AF.Square),
                inc="s")
            ln_state["t2"] = t2
            ln_state["s_m2"] = s_m2
            ln_state["s_mu"] = s_mu
            ln_state["ncol"] = ncol

        def emit_ln_fin():
            ncol = ln_state["ncol"]
            dep("V", "t", ln_state["t2"])
            dep("V", "s", ln_state["s_m2"])
            v_var = run("V", lambda e, ncol=ncol: e.scalar_tensor_tensor(
                out=stat[64:65, 0:ncol], in0=pb[4][0:1, 0:ncol],
                scalar=1.0 / D, in1=stat[32:33, 0:ncol],
                op0=ALU.mult, op1=ALU.subtract), inc="v")
            bank_consumed(4, "v", v_var)
            dep("S", "v", v_var)
            s_l = run("S", lambda e, ncol=ncol: e.activation(
                out=stat[96:97, 0:ncol], in_=stat[64:65, 0:ncol], func=AF.Ln,
                bias=c_eps[0:1, :]), inc="s")
            dep("S", "s", s_l)
            s_r = run("S", lambda e, ncol=ncol: e.activation(
                out=stat[0:1, 272:272 + ncol], in_=stat[96:97, 0:ncol], func=AF.Exp,
                scale=-0.5), inc="s")
            fc, fv = bank[5]["free"]
            dep("T", fc, fv)
            dep("T", "s", s_r)
            tr_ = run("T", lambda e, ncol=ncol: e.matmul(
                out=pb[5][:, 0:ncol], lhsT=ones_row[:],
                rhs=stat[0:1, 272:272 + ncol], start=True, stop=True), inc="t")
            bank[5]["fill"] = ("t", tr_)
            fc, fv = bank[4]["free"]
            dep("T", fc, fv)
            dep("T", "s", ln_state["s_mu"])
            tp_ = run("T", lambda e, ncol=ncol: e.matmul(
                out=pb[4][:, 0:ncol], lhsT=ones_row[:],
                rhs=stat[0:1, 0:ncol], start=True, stop=True),
                inc="t")
            bank[4]["fill"] = ("t", tp_)
            last_tp[0] = tp_
            return tp_

        def rep_views(_, ncol):
            rstd = pb[5][:, 0:ncol].unsqueeze(1).broadcast_to([128, QD, ncol])
            p_ = pb[4][:, 0:ncol].unsqueeze(1).broadcast_to([128, QD, ncol])
            return rstd, p_

        HB = 136
        X_free_h = [("v", 0), ("v", 0)]
        xc_free = ("v", 0)
        xnb_free_h = [("t", 0), ("t", 0)]
        ygb_wfree = ("t", 0)
        fb_wfree = ("t", 0)
        out_dma_prev = ("st", 0)
        hb_ready = ("v", 0)
        zb_free = ("v", 0)
        x2b_free_h = [("v", 0), ("v", 0)]
        G_s = [0]
        G_v = [0]
        win_vfree = {}
        gw = [0]
        vdt_map = {}
        szh_map = {}
        btct_map = {}
        gate_map = {}
        ipb_cols = [cv("ipb", i) for i in range(16)]
        side = deque()
        ffn_tasks = deque()
        cur_ffn_chunk = [0]
        x2scr = [nc.dram_tensor(f"x2scr{i}", [128, QD * TC], F32, kind="Internal").ap()
                 for i in range(2)]
        x2_reload_map = {}

        def vh(fn):
            fn.vheavy = True
            return fn

        def tslice(buf, ncol_layout, t0, t1):
            return buf[:].rearrange("p (q j) -> p q j", j=ncol_layout)[:, :, t0:t1]

        ffn_ok = [False]

        def w1w2_group(cc, olist):
            def f():
                nonlocal hb_ready, fb_wfree
                cur_ffn_chunk[0] = cc
                t1 = t2 = 0

                def flush_pend():
                    if not w12_pend:
                        return 0
                    po, psv = w12_pend.popleft()
                    pso = po % 4
                    dep("V", "s", psv)
                    hv = run("V", lambda e, po=po, pso=pso: e.tensor_tensor(
                        out=hb[:, po * TC:(po + 1) * TC],
                        in0=sw[:, pso * TC:(pso + 1) * TC],
                        in1=sw[:, (4 + pso) * TC:(5 + pso) * TC],
                        op=ALU.mult), inc="v")
                    sw_v[pso] = hv
                    return hv

                if cc == 0:
                    dep("V", "qci0", cur("qci0"))
                    dep("V", "qci1", cur("qci1"))
                dep("T", "v", cur("v"))
                for o in olist:
                    so = o % 2
                    ld1 = load_slab(slabm, slabm_free, so,
                                    wbf["w1T"].rearrange("(k p) j -> p k j", p=128)
                                    [:, :, o * 128:(o + 1) * 128],
                                    deps=cast_done_ld["w1T"])
                    sl2 = ffn_rr[0] % 4
                    ffn_rr[0] += 1
                    ld2 = load_slab(slabf, slabf_free, sl2,
                                    wbf["w2T"].rearrange("(k p) j -> p k j", p=128)
                                    [:, :, o * 128:(o + 1) * 128],
                                    deps=cast_done_ld["w2T"])
                    pairs1 = [((lambda so=so, k=k: slabm[so][:, k * 128:(k + 1) * 128]),
                               (lambda k=k: fb[:, k * TC:(k + 1) * TC]))
                              for k in range(QD)]
                    t1 = mm_group(0, pairs1, pb[0][:, 0:TC], deps=[ld1])
                    slabm_free[so] = ("t", t1)
                    pairs2 = [((lambda sl2=sl2, k=k:
                                slabf[sl2][:, k * 128:(k + 1) * 128]),
                               (lambda k=k: fb[:, k * TC:(k + 1) * TC]))
                              for k in range(QD)]
                    t2 = mm_group(1, pairs2, pb[1][:, 0:TC], deps=[ld2])
                    slabf_free[sl2] = ("t", t2)
                    dep("S", "t", t1)
                    dep("S", "v", sw_v[o % 4])
                    run("S", lambda e, o=o: e.activation(
                        out=sw[:, (o % 4) * TC:(o % 4 + 1) * TC],
                        in_=pb[0][:, 0:TC], func=AF.Silu, scale=cv("w1s", o)),
                        inc="s")
                    dep("S", "t", t2)
                    sv = run("S", lambda e, o=o: e.activation(
                        out=sw[:, (4 + o % 4) * TC:(5 + o % 4) * TC],
                        in_=pb[1][:, 0:TC], func=AF.Identity), inc="s")
                    bank_consumed(0, "s", sv)
                    bank_consumed(1, "s", sv)
                    if len(w12_pend) == 4:
                        flush_pend()
                    w12_pend.append((o, sv))
                if olist[-1] == 31:
                    hv = 0
                    while w12_pend:
                        hv = flush_pend()
                    hb_ready = ("v", hv)
                    fb_wfree = ("t", max(t1, t2))
                    ffn_ok[0] = True
            return f

        HB_BANKS = (2, 3, 7)

        def mk_ffn_tasks(cc, cell):
            tasks = deque()

            def h_round(o, wnm, snm, cellkey):
                def f():
                    bi = HB_BANKS[o % 3]
                    fc, fv = bank[bi]["free"]
                    slots, lds = [], []
                    for half in range(2):
                        sl = ffn_rr[0] % 4
                        ffn_rr[0] += 1
                        slots.append(sl)
                        lds.append(load_slab(
                            slabf, slabf_free, sl,
                            wbf[wnm].rearrange("(k p) j -> p k j", p=128)
                            [:, half * 16:(half + 1) * 16, o * 128:(o + 1) * 128],
                            deps=cast_done_ld[wnm]))
                    dep("T", fc, fv)
                    if wnm == "h1T":
                        dep("T", "v", hb_ready[1])
                    else:
                        dep("T", "s", cell["g1_s"])
                    tv = None
                    rhs_buf = hb if wnm == "h1T" else g1b
                    for half in range(2):
                        dep("T", lds[half][0], lds[half][1])
                        for k in range(16):
                            kk = half * 16 + k
                            a = (kk == 0)
                            bl = (kk == 31)
                            fn = (lambda e, kk=kk, sl=slots[half], bi=bi, a=a, bl=bl,
                                  rb=rhs_buf, k=k:
                                  e.matmul(out=pb[bi][:, 0:TC],
                                           lhsT=slabf[sl][:, k * 128:(k + 1) * 128],
                                           rhs=rb[:, kk * TC:(kk + 1) * TC],
                                           start=a, stop=bl))
                            tv = run("T", fn, inc="t") if bl else run("T", fn)
                    slabf_free[slots[0]] = ("t", tv)
                    slabf_free[slots[1]] = ("t", tv)
                    bank[bi]["fill"] = ("t", tv)
                    cell[cellkey] = max(cell[cellkey], tv)
                    dep("S", "t", tv)
                    if cc == 0 and wnm == "h1T" and o == 0:
                        dep("S", "qco0", cur("qco0"))
                        dep("S", "qco1", cur("qco1"))
                    if wnm == "h2T" and o == 0:
                        dep("S", "t", cell["h1_t"])
                    dst = g1b if wnm == "h1T" else hb
                    sv = run("S", lambda e, o=o, bi=bi, snm=snm, dst=dst:
                             e.activation(
                        out=dst[:, o * TC:(o + 1) * TC], in_=pb[bi][:, 0:TC],
                        func=AF.Gelu, scale=cv(snm, o)), inc="s")
                    bank_consumed(bi, "s", sv)
                    if o == 31:
                        cell["g1_s" if wnm == "h1T" else "g2_s"] = sv
                return f

            def w3_round(o):
                def f():
                    bi = HB_BANKS[o % 3]
                    fc, fv = bank[bi]["free"]
                    slots, lds = [], []
                    for half in range(2):
                        sl = ffn_rr[0] % 4
                        ffn_rr[0] += 1
                        slots.append(sl)
                        lds.append(load_slab(
                            slabf, slabf_free, sl,
                            wbf["w3T"].rearrange("(k p) j -> p k j", p=128)
                            [:, half * 16:(half + 1) * 16, o * 128:(o + 1) * 128],
                            deps=cast_done_ld["w3T"]))
                    dep("T", fc, fv)
                    dep("T", "s", cell["g2_s"])
                    tv = None
                    for half in range(2):
                        dep("T", lds[half][0], lds[half][1])
                        for k in range(16):
                            kk = half * 16 + k
                            a = (kk == 0)
                            bl = (kk == 31)
                            fn = (lambda e, kk=kk, sl=slots[half], bi=bi, a=a, bl=bl,
                                  k=k:
                                  e.matmul(out=pb[bi][:, 0:TC],
                                           lhsT=slabf[sl][:, k * 128:(k + 1) * 128],
                                           rhs=hb[:, kk * TC:(kk + 1) * TC],
                                           start=a, stop=bl))
                            tv = run("T", fn, inc="t") if bl else run("T", fn)
                    slabf_free[slots[0]] = ("t", tv)
                    slabf_free[slots[1]] = ("t", tv)
                    bank[bi]["fill"] = ("t", tv)
                    dep("S", "t", tv)
                    if o == 0:
                        cn, vv = out_dma_prev
                        dep("S", cn, vv)
                    sv = run("S", lambda e, o=o, bi=bi: e.activation(
                        out=Fb[:, o * TC:(o + 1) * TC], in_=pb[bi][:, 0:TC],
                        func=AF.Identity, scale=cv("w3s", o)), inc="s")
                    bank_consumed(bi, "s", sv)
                return f

            for o in range(32):
                tasks.append(h_round(o, "h1T", "h1s", "h1_t"))
            for o in range(32):
                tasks.append(h_round(o, "h2T", "h2s", "h2_t"))
            for o in range(QD):
                tasks.append(w3_round(o))
            return tasks

        def ln_apply_to_G(src_fn, ncol, t0, t1, out2=None):
            n = t1 - t0
            rstd = pb[5][:, 0:n].unsqueeze(1).broadcast_to([128, QD, n])
            mu_b = pb[4][:, 0:n].unsqueeze(1).broadcast_to([128, QD, n])
            dep("V", "t", last_tp[0])
            dep("V", "s", G_s[0])
            run("V", lambda e: e.tensor_tensor(
                out=tslice(G, TC + 1, t0, t1), in0=src_fn(), in1=mu_b,
                op=ALU.subtract))
            dst = out2 if out2 is not None else (lambda: tslice(G, TC + 1, t0, t1))
            vv = run("V", lambda e: e.tensor_tensor(
                out=dst(), in0=tslice(G, TC + 1, t0, t1),
                in1=rstd, op=ALU.mult), inc="v")
            G_v[0] = vv
            bank_consumed(5, "v", vv)
            bank_consumed(4, "v", vv)
            return vv

        def post_half_closures(c, h):
            t0, t1 = (0, HB) if h == 0 else (HB, TC)
            n = t1 - t0
            cls = []

            def c_outproj_a(opart):
                def f():
                    nonlocal ygb_wfree
                    dep_v = gate_map[(c, h)]
                    op_t = 0
                    for o in opart:
                        so = o % 2
                        ldv = load_slab(slabm, slabm_free, so,
                                        wbf["outT"].rearrange("(k p) j -> p k j", p=128)
                                        [:, :, o * 128:(o + 1) * 128],
                                        deps=cast_done_ld["outT"])
                        bi = o % 2
                        pairs = [((lambda so=so, k=k: slabm[so][:, k * 128:(k + 1) * 128]),
                                  (lambda k=k, t0=t0, t1=t1:
                                   xsb[:, k * TC + t0:k * TC + t1]))
                                 for k in range(QD)]
                        mmdeps = [ldv, ("v", dep_v)]
                        if o == 0 and h == 0:
                            mmdeps.append(x2b_free_h[0])
                        if o == 0 and h == 1:
                            mmdeps.append(x2b_free_h[1])
                        tv = mm_group(bi, pairs, pb[bi][:, 0:n], deps=mmdeps)
                        slabm_free[so] = ("t", tv)
                        op_t = tv
                        dep("S", "t", tv)
                        cn, vv = x2b_free_h[h]
                        dep("S", cn, vv)
                        sv = run("S", lambda e, o=o, bi=bi, t0=t0, t1=t1, n=n:
                                 e.activation(
                            out=x2[:, o * TC + t0:o * TC + t1], in_=pb[bi][:, 0:n],
                            func=AF.Identity, bias=cv("outb", o)), inc="s")
                        bank_consumed(bi, "s", sv)
                    if h == 1 and opart[-1] == 7:
                        ygb_wfree = ("t", op_t)
                return f
            cls.append(c_outproj_a([0, 1, 2, 3]))
            cls.append(c_outproj_a([4, 5, 6, 7]))

            def c_mr():
                def f():
                    dep("V", "s", cur("s"))
                    run("V", lambda e: e.tensor_tensor(
                        out=tslice(x2, TC, t0, t1), in0=tslice(x2, TC, t0, t1),
                        in1=tslice(xnb, TC + 1, t0, t1), op=ALU.add), inc="v")
                return f
            cls.append(vh(c_mr()))

            ln_bc = [0]

            def c_stats(src_fn, extra=()):
                def f():
                    emit_ln(src_fn, n, stat_cast,
                            deps=list(extra) + [("v", cur("v"))])
                return f

            def c_mixapply():
                def f():
                    emit_ln_fin()
                    ln_apply_to_G(lambda: tslice(x2, TC, t0, t1), n, t0, t1)
                    dep("V", "s", G_s[0])
                    vv = None
                    for q in range(QD):
                        vv = run("V", lambda e, q=q: e.scalar_tensor_tensor(
                            out=x2[:, q * TC + t0:q * TC + t1],
                            in0=G[:, q * (TC + 1) + t0:q * (TC + 1) + t1],
                            scalar=cv("lng", q),
                            in1=cv("lnb", q).broadcast_to([128, n]),
                            op0=ALU.mult, op1=ALU.add), inc="v")
                    G_v[0] = vv
                    bank_consumed(5, "v", vv)
                    bank_consumed(4, "v", vv)
                return f

            def c_ln2apply():
                def f():
                    nonlocal X_free_h
                    emit_ln_fin()
                    ln_apply_to_G(lambda: tslice(x2, TC, t0, t1), n, t0, t1)
                    vv = run("V", lambda e: e.tensor_tensor(
                        out=tslice(x2, TC, t0, t1), in0=tslice(G, TC + 1, t0, t1),
                        in1=tslice(X, TC + 1, t0, t1), op=ALU.add), inc="v")
                    G_v[0] = vv
                    X_free_h[h] = ("v", vv)
                    bank_consumed(5, "v", vv)
                    bank_consumed(4, "v", vv)
                return f

            def c_ln3apply():
                def f():
                    nonlocal x2b_free_h
                    emit_ln_fin()
                    cn, vv0 = fb_wfree
                    dep("V", cn, vv0)
                    vv = ln_apply_to_G(lambda: tslice(x2, TC, t0, t1), n, t0, t1,
                                       out2=lambda: tslice(fb, TC, t0, t1))
                    bank_consumed(5, "v", vv)
                    bank_consumed(4, "v", vv)
                    clsn = "qci0" if c % 2 == 0 else "qci1"
                    dep("Q", clsn, cur(clsn))
                    dep("Q", "v", vv)
                    dep("Q", "s", cur("s"))
                    sv = run("Q", lambda e, c=c, t0=t0, t1=t1: e.dma_start(
                        out=x2scr[c % 2].rearrange("p (q j) -> p q j", j=TC)
                        [:, :, t0:t1],
                        in_=tslice(x2, TC, t0, t1)), inc=clsn, inck=16)
                    x2b_free_h[h] = (clsn, sv)
                    x2_reload_map.setdefault(c, []).append((clsn, sv))
                return f

            cls.append(c_stats(lambda: tslice(x2, TC, t0, t1)))
            cls.append(vh(c_mixapply()))
            cls.append(c_stats(lambda: tslice(x2, TC, t0, t1)))
            cls.append(vh(c_ln2apply()))
            cls.append(c_stats(lambda: tslice(x2, TC, t0, t1)))
            cls.append(vh(c_ln3apply()))
            return cls

        def pre_half_closures(cn, h):
            t0, t1 = (0, HB) if h == 0 else (HB, TC + 1)
            n = t1 - t0
            last = (cn == NCH - 1)
            cv0, cv1 = (0, 133) if h == 0 else (133, TC)
            cls = []

            ln_bc1 = [0]

            def c_xload():
                def f():
                    cnf, vvf = X_free_h[h]
                    dep("Q", cnf, vvf)
                    qcl = "qx"
                    dep("Q", qcl, cur(qcl))
                    tload = t1 if not (last and h == 1) else TC
                    xl = run("Q", lambda e, cn=cn, t0=t0, tload=tload: e.dma_start(
                        out=tslice(X, TC + 1, t0, tload),
                        in_=xT.rearrange("(q p) t -> p q t", p=128)
                        [:, :, cn * TC + t0:cn * TC + tload]), inc=qcl, inck=16)
                    if last and h == 1:
                        dep("V", qcl, xl)
                        xl2 = run("V", lambda e: e.memset(
                            tslice(X, TC + 1, TC, TC + 1), 0.0), inc="v")
                        emit_ln(lambda: tslice(X, TC + 1, t0, t1), n,
                                stat_cast, deps=[("v", xl2)])
                    else:
                        emit_ln(lambda: tslice(X, TC + 1, t0, t1), n,
                                stat_cast, deps=[(qcl, xl)])
                return f
            cls.append(c_xload())

            def c_ln1apply():
                def f():
                    nonlocal xnb_free_h
                    emit_ln_fin()
                    cnf, vvf = xnb_free_h[h]
                    dep("V", cnf, vvf)
                    vv = ln_apply_to_G(lambda: tslice(X, TC + 1, t0, t1), n, t0, t1,
                                       out2=lambda: tslice(xnb, TC + 1, t0, t1))
                    bank_consumed(5, "v", vv)
                    bank_consumed(4, "v", vv)
                    xnb_v[h] = vv
                return f
            cls.append(vh(c_ln1apply()))

            ipev = [0] * 8
            sp_sl = [0]
            xnb_v = [0, 0]

            def c_inproj(opart):
                def f():
                    s_x = cur("s")
                    v_x = xnb_v[h]
                    for o in opart:
                        so = o % 2
                        ldv = load_slab(slabm, slabm_free, so,
                                        wbf["ipT"].rearrange("(k p) j -> p k j", p=128)
                                        [:, :, o * 128:(o + 1) * 128],
                                        deps=cast_done_ld["ipT"])
                        bi = o % 2
                        pairs = [((lambda so=so, k=k: slabm[so][:, k * 128:(k + 1) * 128]),
                                  (lambda k=k, t0=t0, t1=t1:
                                   xnb[:, k * (TC + 1) + t0:k * (TC + 1) + t1]))
                                 for k in range(QD)]
                        mmdeps = [ldv, ("s", s_x), ("v", v_x)]
                        if o == opart[0]:
                            mmdeps.append(xc_free if o < 8 else zb_free)
                        tv = mm_group(bi, pairs, pb[bi][:, 0:n], deps=mmdeps)
                        slabm_free[so] = ("t", tv)
                        dep("S", "t", tv)
                        if o < 8:
                            cnf, vvf = xc_free
                            dep("S", cnf, vvf)
                            sv = run("S", lambda e, o=o, bi=bi, t0=t0, n=n:
                                     e.activation(
                                out=xc[:, o * (TC + 3) + 2 + t0:
                                       o * (TC + 3) + 2 + t0 + n],
                                in_=pb[bi][:, 0:n], func=AF.Identity,
                                bias=ipb_cols[o]), inc="s")
                            ipev[o] = sv
                        else:
                            cnf, vvf = zb_free
                            dep("S", cnf, vvf)
                            nz = min(t1, TC) - t0
                            sv = run("S", lambda e, o=o, bi=bi, t0=t0, nz=nz:
                                     e.activation(
                                out=zb[:, (o - 8) * TC + t0:(o - 8) * TC + t0 + nz],
                                in_=pb[bi][:, 0:nz], func=AF.Identity,
                                bias=ipb_cols[o]), inc="s")
                        bank_consumed(bi, "s", sv)
                    if opart[-1] == 15:
                        xnb_free_h[h] = ("t", cur("t"))
                return f
            cls.append(c_inproj(list(range(0, 8))))
            cls.append(c_inproj(list(range(8, 16))))

            def c_conv():
                def f():
                    nonlocal xc_free
                    if h == 0:
                        cnf, vvf = xc_free
                        dep("S", cnf, vvf)
                        sh = run("S", lambda e: e.copy(
                            out=xc[:].rearrange("p (q j) -> p q j", j=TC + 3)[:, :, 0:2],
                            in_=halo[:].rearrange("p (q j) -> p q j", j=2)), inc="s")
                        dep("V", "s", sh)
                    if last and h == 1:
                        dep("V", "s", ipev[7])
                        run("V", lambda e: e.memset(
                            xc[:].rearrange("p (q j) -> p q j", j=TC + 3)
                            [:, :, TC + 2:TC + 3], 0.0), inc="v")
                    nn = cv1 - cv0
                    dep("V", "s", G_s[0])
                    for q in range(QD):
                        dep("V", "s", ipev[q])
                        base = q * (TC + 3)
                        for k in range(4):
                            src = xc[:, base + cv0 + k:base + cv0 + k + nn]
                            wcol = cv("convw", q * 4 + k)
                            if k == 0:
                                run("V", lambda e, src=src, wcol=wcol, q=q, nn=nn:
                                    e.scalar_tensor_tensor(
                                        out=sa_s[:, 0:nn], in0=src, scalar=wcol,
                                        in1=cv("convb", q).broadcast_to([128, nn]),
                                        op0=ALU.mult, op1=ALU.add))
                            elif k < 3:
                                srcacc = sa_s[:, 0:nn] if k == 1 else raw_s[:, 0:nn]
                                dst = raw_s[:, 0:nn] if k == 1 else sa_s[:, 0:nn]
                                run("V", lambda e, src=src, wcol=wcol, i1=srcacc, d=dst:
                                    e.scalar_tensor_tensor(out=d, in0=src, scalar=wcol,
                                                           in1=i1, op0=ALU.mult,
                                                           op1=ALU.add))
                            else:
                                run("V", lambda e, src=src, wcol=wcol, q=q, nn=nn:
                                    e.scalar_tensor_tensor(
                                        out=G[:, q * (TC + 1) + cv0:
                                              q * (TC + 1) + cv0 + nn],
                                        in0=src, scalar=wcol, in1=sa_s[:, 0:nn],
                                        op0=ALU.mult, op1=ALU.add))
                    if h == 1:
                        vv = run("V", lambda e: e.tensor_copy(
                            out=halo[:].rearrange("p (q j) -> p q j", j=2),
                            in_=xc[:].rearrange("p (q j) -> p q j", j=TC + 3)
                            [:, :, TC:TC + 2]), inc="v")
                        xc_free = ("v", vv)
                        G_v[0] = vv
                    else:
                        vv = run("V", lambda e: e.engine_nop(), inc="v")
                        G_v[0] = vv
                    dep("S", "v", vv)
                    s_xs = run("S", lambda e, cv0=cv0, nn=nn: e.activation(
                        out=tslice(xs, TC, cv0, cv0 + nn),
                        in_=tslice(G[:, 0:QD * (TC + 1)], TC + 1, cv0, cv0 + nn),
                        func=AF.Silu), inc="s")
                    G_s[0] = s_xs
                    z1 = min(t1, TC)
                    s_sz = run("S", lambda e, t0=t0, z1=z1: e.activation(
                        out=tslice(szh, TC, t0, z1), in_=tslice(zb, TC, t0, z1),
                        func=AF.Silu), inc="s")
                    szh_map[(cn, h)] = s_sz
                return f
            cls.append(vh(c_conv()))

            def c_bc():
                def f():
                    nn = cv1 - cv0
                    s_c = run("S", lambda e, cv0=cv0, nn=nn: e.copy(
                        out=tslice(xsc, TC, cv0, cv0 + nn),
                        in_=tslice(xs, TC, cv0, cv0 + nn)), inc="s")
                    blk = h
                    fc, fv = bank[6]["free"]
                    dep("T", fc, fv)
                    dep("T", "s", s_c)
                    dep("T", "qmisc", bcldv)
                    tv = None
                    for k in range(QD):
                        a, bl = (k == 0), (k == QD - 1)
                        fn = (lambda e, blk=blk, k=k, a=a, bl=bl:
                              e.matmul(out=pb[6][:, 0:64],
                                       lhsT=xsc[:, k * TC + blk * 128:
                                                k * TC + blk * 128 + 128],
                                       rhs=bcw[:, k * 64:(k + 1) * 64],
                                       start=a, stop=bl))
                        tv = run("T", fn, inc="t") if bl else run("T", fn)
                    bank[6]["fill"] = ("t", tv)
                    dep("V", "t", tv)
                    v_bc = run("V", lambda e: e.tensor_tensor(
                        out=bcsb[:].rearrange("p (a b) -> p a b", b=64)[:, 0:1, :],
                        in0=pb[6][:, 0:64].unsqueeze(1),
                        in1=cv("bcb").unsqueeze(1).broadcast_to([128, 1, 64]),
                        op=ALU.add), inc="v")
                    bank_consumed(6, "v", v_bc)
                    dep("Q", "v", v_bc)
                    qbt_cls = "qbt0" if cn % 2 == 0 else "qbt1"
                    dep("Q", qbt_cls, cur(qbt_cls))
                    q_bt = run("Q", lambda e, cn=cn, blk=blk: e.dma_start(
                        out=btct[cn % 2].rearrange("(b p j) -> p b j", p=128, j=64)
                        [:, blk:blk + 1, :],
                        in_=bcsb[:].rearrange("p (a b) -> p a b", b=64)[:, 0:1, :]),
                        inc=qbt_cls, inck=16)
                    btct_map[(cn, h)] = (qbt_cls, q_bt)
                return f
            cls.append(c_bc())

            def c_dt():
                def f():
                    b0 = h * 128
                    for o in range(QD):
                        so = o % 2
                        ldv = load_slab(slabm, slabm_free, so,
                                        wbf["dtT"].rearrange("(k p) j -> p k j", p=128)
                                        [:, :, o * 128:(o + 1) * 128],
                                        deps=cast_done_ld["dtT"])
                        bi = o % 2
                        pairs = [((lambda so=so, k=k: slabm[so][:, k * 128:(k + 1) * 128]),
                                  (lambda k=k, b0=b0: xnb[:, k * (TC + 1) + b0:
                                                          k * (TC + 1) + b0 + 128]))
                                 for k in range(QD)]
                        tv = mm_group(bi, pairs, pb[bi][:, 0:128], deps=[ldv])
                        slabm_free[so] = ("t", tv)
                        dep("S", "t", tv)
                        sv = run("S", lambda e, o=o, bi=bi, b0=b0: e.activation(
                            out=zb[:, o * TC + b0:o * TC + b0 + 128],
                            in_=pb[bi][:, 0:128], func=AF.Identity,
                            bias=cv("dtbias", o)), inc="s")
                        bank_consumed(bi, "s", sv)
                    if h == 1:
                        xnb_free_h[0] = ("t", cur("t"))
                return f
            cls.append(c_dt())

            def c_softplus():
                def f():
                    nonlocal zb_free
                    b0 = h * 128
                    cn_, vv_ = out_dma_prev
                    dep("S", cn_, vv_)
                    dep("S", "v", G_v[0])
                    s_ab = run("S", lambda e, b0=b0: e.activation(
                        out=tslice(G[:, 0:QD * (TC + 1)], TC + 1, b0, b0 + 128),
                        in_=tslice(zb, TC, b0, b0 + 128), func=AF.Abs), inc="s")
                    dep("S", "s", s_ab)
                    s_ex = run("S", lambda e, b0=b0: e.activation(
                        out=dxb_t[:, b0 * QD:(b0 + 128) * QD],
                        in_=tslice(G[:, 0:QD * (TC + 1)], TC + 1, b0, b0 + 128),
                        func=AF.Exp, scale=-1.0), inc="s")
                    dep("S", "s", s_ex)
                    s_l = run("S", lambda e, b0=b0: e.activation(
                        out=tslice(G[:, 0:QD * (TC + 1)], TC + 1, b0, b0 + 128),
                        in_=dxb_t[:, b0 * QD:(b0 + 128) * QD], func=AF.Ln,
                        bias=c_one[:]), inc="s")
                    G_s[0] = s_l
                    sp_sl[0] = s_l
                return f

            def c_softplus_v():
                def f():
                    nonlocal zb_free
                    b0 = h * 128
                    s_l = sp_sl[0]
                    dep("V", "s", s_l)
                    run("V", lambda e, b0=b0: e.scalar_tensor_tensor(
                        out=tslice(G[:, 0:QD * (TC + 1)], TC + 1, b0, b0 + 128),
                        in0=tslice(zb, TC, b0, b0 + 128), scalar=0.0, op0=ALU.max,
                        in1=tslice(G[:, 0:QD * (TC + 1)], TC + 1, b0, b0 + 128),
                        op1=ALU.add))
                    vz = run("V", lambda e, b0=b0: e.tensor_scalar(
                        out=dtb_t[:, b0 * QD:(b0 + 128) * QD].rearrange(
                            "p (t q) -> p t q", q=QD).rearrange("p t q -> p q t"),
                        in0=tslice(G[:, 0:QD * (TC + 1)], TC + 1, b0, b0 + 128),
                        scalar1=0.2, scalar2=1e-3, op0=ALU.min, op1=ALU.max), inc="v")
                    zb_free = ("v", vz)
                    vd = run("V", lambda e, b0=b0: e.tensor_tensor(
                        out=dxb_t[:, b0 * QD:(b0 + 128) * QD].rearrange(
                            "p (t q) -> p t q", q=QD).rearrange("p t q -> p q t"),
                        in0=dtb_t[:, b0 * QD:(b0 + 128) * QD].rearrange(
                            "p (t q) -> p t q", q=QD).rearrange("p t q -> p q t"),
                        in1=tslice(xs, TC, b0, b0 + 128), op=ALU.mult), inc="v")
                    G_v[0] = vd
                    vdt_map[(cn, h)] = vd
                return f
            cls.append(c_softplus())
            cls.append(vh(c_softplus_v()))
            return cls

        for c in range(NCH + 1):
            cm1 = c - 1
            have_cur = c < NCH
            have_prev = cm1 >= 0

            cell = {"g1_s": 0, "g2_s": 0, "h1_t": 0, "h2_t": 0}
            if have_prev:
                ffn_tasks = mk_ffn_tasks(cm1, cell)
                ffn_ok[0] = False
            else:
                ffn_tasks = cast_tasks
                ffn_ok[0] = True

            if c == 0:
                for f in pre_half_closures(0, 0):
                    f()
                for f in pre_half_closures(0, 1):
                    f()

            if have_cur:
                g0_chunk = gw[0]

                def emit_am(w, c=c, g0_chunk=g0_chunk):
                    g = g0_chunk + w
                    b3 = g % 3
                    cnf, vvf = win_vfree.get(g - 3, ("v", 0))
                    dep("P", cnf, vvf)
                    dep("P", "v", vdt_map[(c, w // 16)])
                    pa = run("P", lambda e, w=w, b3=b3: e.tensor_tensor(
                        out=amb[b3][:].rearrange("p (t q n) -> p t q n", q=QD, n=NST),
                        in0=dtb_t[:, w * W * QD:(w + 1) * W * QD].rearrange(
                            "p (t q) -> p t q", q=QD).unsqueeze(3).broadcast_to(
                            [128, W, QD, NST]),
                        in1=Ap[:].rearrange("p (q n) -> p q n", n=NST).unsqueeze(1)
                        .broadcast_to([128, W, QD, NST]), op=ALU.mult), inc="p")
                    dep("S", cnf, vvf)
                    dep("S", "p", pa)
                    se = run("S", lambda e, b3=b3: e.activation(
                        out=abar[b3][:], in_=amb[b3][:], func=AF.Exp, scale=-1.0),
                        inc="s")
                    return se

                def emit_u(w, c=c, g0_chunk=g0_chunk):
                    g = g0_chunk + w
                    b2 = g % 2
                    cnf, vvf = win_vfree.get(g - 2, ("v", 0))
                    bcls, bval = btct_map[(c, w // 16)]
                    dep("S", cnf, vvf)
                    dep("S", bcls, bval)
                    sbr = ("sbr0", "sbr1")[b2]
                    scr = ("scr0", "scr1")[b2]
                    db = run("S", lambda e, w=w, b2=b2, c=c: e.dma_start(
                        out=brep[b2][:].rearrange("p (t j) -> p t j", j=NST),
                        in_=btct[c % 2].rearrange("(t j) -> t j", j=64)
                        [w * W:(w + 1) * W, 0:NST].unsqueeze(0)
                        .broadcast_to([128, W, NST])), inc=sbr, inck=16)
                    dc = run("S", lambda e, w=w, b2=b2, c=c: e.dma_start(
                        out=crep[b2][:].rearrange("p (t j) -> p t j", j=NST),
                        in_=btct[c % 2].rearrange("(t j) -> t j", j=64)
                        [w * W:(w + 1) * W, NST:64].unsqueeze(0)
                        .broadcast_to([128, W, NST])), inc=scr, inck=16)
                    dep("P", cnf, vvf)
                    dep("P", "v", vdt_map[(c, w // 16)])
                    dep("P", sbr, db)
                    pu = run("P", lambda e, w=w, b2=b2: e.tensor_tensor(
                        out=ub[b2][:].rearrange("p (t q n) -> p t q n", q=QD, n=NST),
                        in0=dxb_t[:, w * W * QD:(w + 1) * W * QD].rearrange(
                            "p (t q) -> p t q", q=QD).unsqueeze(3).broadcast_to(
                            [128, W, QD, NST]),
                        in1=brep[b2][:].rearrange("p (t n) -> p t n", n=NST)
                        .unsqueeze(2).broadcast_to([128, W, QD, NST]),
                        op=ALU.mult), inc="p")
                    return pu, dc, scr

                amq = [emit_am(0), emit_am(1)]
                uq = [emit_u(0), emit_u(1)]
                for w in range(NWIN):
                    g = gw[0]
                    b3 = g % 3
                    b2 = g % 2
                    se = amq.pop(0)
                    pu, dc, scr = uq.pop(0)
                    dep("V", "s", se)
                    dep("V", "p", pu)
                    v_mid = None
                    for tr in range(W):
                        prev = (tr - 1) % W
                        run("V", lambda e, prev=prev, tr=tr, b3=b3: e.tensor_tensor(
                            out=sa_s[:], in0=chist[:, prev * DN:(prev + 1) * DN],
                            in1=abar[b3][:, tr * DN:(tr + 1) * DN], op=ALU.mult))
                        run("V", lambda e, tr=tr, b2=b2: e.tensor_tensor(
                            out=raw_s[:], in0=sa_s[:],
                            in1=ub[b2][:, tr * DN:(tr + 1) * DN], op=ALU.add))
                        v_mid = run("V", lambda e, tr=tr: e.scalar_tensor_tensor(
                            out=chist[:, tr * DN:(tr + 1) * DN], in0=raw_s[:],
                            scalar=10.0, in1=ncap10[:].broadcast_to([128, DN]),
                            op0=ALU.min, op1=ALU.max),
                            inc="v" if tr == W - 1 else None)
                    dep("V", scr, dc)
                    run("V", lambda e, b3=b3, b2=b2: e.tensor_tensor(
                        out=amb[b3][:].rearrange("p (t q n) -> p t q n", q=QD, n=NST),
                        in0=chist[:].rearrange("p (t q n) -> p t q n", q=QD, n=NST),
                        in1=crep[b2][:].rearrange("p (t n) -> p t n", n=NST)
                        .unsqueeze(2).broadcast_to([128, W, QD, NST]),
                        op=ALU.mult))
                    run("V", lambda e, b3=b3: e.tensor_reduce(
                        out=ysw[:], in_=amb[b3][:].rearrange("p (tq n) -> p tq n",
                                                             n=NST),
                        axis=AX.X, op=ALU.add))
                    hw = 0 if w < 17 else 1
                    dep("V", "s", szh_map[(c, hw)])
                    if w == 17:
                        cnf, vvf = ygb_wfree
                        dep("V", cnf, vvf)
                    vr = run("V", lambda e, w=w: e.tensor_tensor(
                        out=xsb[:].rearrange("p (q j) -> p q j", j=TC)
                        [:, :, w * W:(w + 1) * W].rearrange("p q j -> p j q"),
                        in0=ysw[:].rearrange("p (t q) -> p t q", q=QD),
                        in1=szh[:].rearrange("p (q j) -> p q j", j=TC)
                        [:, :, w * W:(w + 1) * W].rearrange("p q j -> p j q"),
                        op=ALU.mult), inc="v")
                    win_vfree[g] = ("v", vr)
                    if w == 16:
                        gate_map[(c, 0)] = vr
                    if w == NWIN - 1:
                        gate_map[(c, 1)] = vr
                    if w + 2 < NWIN:
                        amq.append(emit_am(w + 2))
                        uq.append(emit_u(w + 2))
                    gw[0] += 1
                    if w == 17:
                        postA = post_half_closures(c, 0)
                        preA = (pre_half_closures(c + 1, 0)
                                if c + 1 < NCH else [None] * 9)
                        order = [postA[0], postA[1], postA[2], postA[3],
                                 postA[4], postA[5], postA[6], preA[0],
                                 preA[1], preA[2], preA[3], postA[7],
                                 postA[8], preA[4], preA[5], preA[6],
                                 preA[7], preA[8]]
                        for f_ in order:
                            if f_ is not None:
                                side.append(f_)
                    for _ in range(2):
                        if side:
                            side.popleft()()
                    nff = 2 if c == 0 else FFN_INTERLEAVE
                    if ffn_ok[0]:
                        for _ in range(nff):
                            if ffn_tasks:
                                ffn_tasks.popleft()()
                while ffn_tasks:
                    ffn_tasks.popleft()()

            def mk_ln4(cm1):
                bc4c = [0]

                def f_stats():
                    s_fdone = cur("s")
                    emit_ln(lambda: qv(Fb[:], TC), TC, stat_cast,
                            deps=[("t", cur("t")), ("s", s_fdone)])
                    bc4c.append(s_fdone)

                def f_apply():
                    nonlocal out_dma_prev
                    emit_ln_fin()
                    dep("V", "s", bc4c[1])
                    dep("V", "s", G_s[0])
                    dep("V", "t", last_tp[0])
                    rstd4 = pb[5][:, 0:TC].unsqueeze(1).broadcast_to([128, QD, TC])
                    mu4 = pb[4][:, 0:TC].unsqueeze(1).broadcast_to([128, QD, TC])
                    run("V", lambda e, m=mu4: e.tensor_tensor(
                        out=qv(G[:, 0:QD * TC], TC), in0=qv(Fb[:], TC), in1=m,
                        op=ALU.subtract))
                    v_t1 = run("V", lambda e, r=rstd4: e.tensor_tensor(
                        out=qv(G[:, 0:QD * TC], TC), in0=qv(G[:, 0:QD * TC], TC),
                        in1=r, op=ALU.mult), inc="v")
                    G_v[0] = v_t1
                    bank_consumed(5, "v", v_t1)
                    bank_consumed(4, "v", v_t1)
                    for cls_, vv_ in x2_reload_map.get(cm1, []):
                        dep("Q", cls_, vv_)
                    dep("Q", "v", v_t1)
                    rcl = "qci0" if cm1 % 2 == 0 else "qci1"
                    dep("Q", rcl, cur(rcl))
                    rl = run("Q", lambda e, cm1=cm1: e.dma_start(
                        out=Fb[:], in_=x2scr[cm1 % 2]), inc=rcl, inck=16)
                    dep("V", rcl, rl)
                    v_fin = run("V", lambda e: e.tensor_tensor(
                        out=Fb[:], in0=G[:, 0:QD * TC], in1=Fb[:], op=ALU.add),
                        inc="v")
                    G_v[0] = v_fin
                    dep("Q", "v", v_fin)
                    sv = run("Q", lambda e, cm1=cm1: e.dma_start(
                        out=oT.rearrange("(q p) t -> p q t", p=128)
                        [:, :, cm1 * TC:(cm1 + 1) * TC], in_=qv(Fb[:], TC)),
                        inc="st", inck=16)
                    out_dma_prev = ("st", sv)
                return [f_stats, vh(f_apply)]

            if not have_cur:
                if have_prev:
                    while side:
                        side.popleft()()
                    while ffn_tasks:
                        ffn_tasks.popleft()()
                    for f_ in mk_ln4(cm1):
                        f_()
                break

            postB = post_half_closures(c, 1)
            if have_prev:
                ln4s, ln4a = mk_ln4(cm1)
                side.append(postB.pop(0))
                side.append(ln4s)
                side.append(postB.pop(0))
                side.append(ln4a)
            for f_ in postB:
                side.append(f_)
            w1w2s = [w1w2_group(c, list(range(i, i + 8))) for i in range(0, 32, 8)]
            preB = (pre_half_closures(c + 1, 1) if c + 1 < NCH else [None] * 9)
            orderB = [w1w2s[0], w1w2s[1], preB[0], preB[1], w1w2s[2], preB[2],
                      preB[3], w1w2s[3], preB[4], preB[5], preB[6], preB[7],
                      preB[8]]
            for f_ in orderB:
                if f_ is not None:
                    side.append(f_)

        dep("Q", "st", cur("st"))

        @block.sync
        def _(sync):
            for fn in prog["Q"]:
                fn(sync)

        @block.vector
        def _(vector):
            for fn in prog["V"]:
                fn(vector)

        @block.scalar
        def _(scalar):
            for fn in prog["S"]:
                fn(scalar)

        @block.tensor
        def _(tensor):
            for fn in prog["T"]:
                fn(tensor)

        @block.gpsimd
        def _(gpsimd):
            for fn in prog["P"]:
                fn(gpsimd)

    return nc


def _prep_consts(inp):
    cols = np.zeros((128, CK), dtype=np.float32)

    def put(name, arr):
        o = _offs[name]
        cols[:, o:o + arr.shape[1]] = arr
    put("ipb", np.ascontiguousarray(inp["L0_attn_in_proj_b"].reshape(16, 128).T))
    cw = inp["L0_attn_conv_w"][:, 0, :]
    put("convw", np.ascontiguousarray(
        cw.reshape(8, 128, 4).transpose(1, 0, 2).reshape(128, 32)))
    put("convb", np.ascontiguousarray(inp["L0_attn_conv_b"].reshape(8, 128).T))
    put("dtbias", np.ascontiguousarray(inp["L0_attn_dt_b"].reshape(8, 128).T))
    put("outb", np.ascontiguousarray(inp["L0_attn_out_b"].reshape(8, 128).T))
    put("lng", np.ascontiguousarray(inp["L0_attn_ln_g"].reshape(8, 128).T))
    put("lnb", np.ascontiguousarray(inp["L0_attn_ln_beta"].reshape(8, 128).T))
    al = inp["L0_attn_A_log"]
    put("alog", np.ascontiguousarray(
        al.reshape(8, 128, 32).transpose(1, 0, 2).reshape(128, 256)))
    put("w1s", np.ascontiguousarray(inp["L0_ffn_w1_s"].reshape(32, 128).T))
    put("w2s", np.ascontiguousarray(inp["L0_ffn_w2_s"].reshape(32, 128).T))
    put("h1s", np.ascontiguousarray(inp["L0_ffn_h1_s"].reshape(32, 128).T))
    put("h2s", np.ascontiguousarray(inp["L0_ffn_h2_s"].reshape(32, 128).T))
    put("w3s", np.ascontiguousarray(inp["L0_ffn_w3_s"].reshape(8, 128).T))
    bcb = np.concatenate([inp["L0_attn_B_b"], inp["L0_attn_C_b"]])
    put("bcb", np.tile(bcb[None, :], (128, 1)))
    return cols


def kernel(**inp):
    inp = {k: np.asarray(v) for k, v in inp.items()}
    x = inp["L0_attn_x"]
    consts = _prep_consts(inp)
    wT = {
        "ipT": np.ascontiguousarray(inp["L0_attn_in_proj_w"].T),
        "dtT": np.ascontiguousarray(inp["L0_attn_dt_w"].T),
        "outT": np.ascontiguousarray(inp["L0_attn_out_w"].T),
        "bcT": np.ascontiguousarray(
            np.concatenate([inp["L0_attn_B_w"], inp["L0_attn_C_w"]], axis=0).T),
        "w1T": np.ascontiguousarray(inp["L0_ffn_w1_w"].T),
        "w2T": np.ascontiguousarray(inp["L0_ffn_w2_w"].T),
        "h1T": np.ascontiguousarray(inp["L0_ffn_h1_w"].T),
        "h2T": np.ascontiguousarray(inp["L0_ffn_h2_w"].T),
        "w3T": np.ascontiguousarray(inp["L0_ffn_w3_w"].T),
    }
    nc = build_program()
    in_maps = []
    for i in range(8):
        m = {"xT": np.ascontiguousarray(x[i].T), "consts": consts}
        m.update(wT)
        in_maps.append(m)
    res = run_bass_kernel_spmd(nc, in_maps, core_ids=list(range(8)))
    out = np.stack([np.ascontiguousarray(r["oT"].T) for r in res.results], axis=0)
    return out


# version 25
# speedup vs baseline: 1.0387x; 1.0240x over previous
import os
os.environ.setdefault("GAUGE_NTFF_IGNORE_DMA", "1")

import numpy as np
from contextlib import ExitStack
from collections import deque

import concourse.bass as bass
import concourse.mybir as mybir
from concourse.bass_utils import run_bass_kernel_spmd

F32 = mybir.dt.float32
BF16 = mybir.dt.bfloat16
FP16 = mybir.dt.float16
ALU = mybir.AluOpType
AF = mybir.ActivationFunctionType
AX = mybir.AxisListType

D = 1024
T = 4096
TC = 256
NCH = T // TC
W = 8
NWIN = TC // W
QD = 8
NST = 32
DN = QD * NST
EPS = 1e-5

_offs = {}
_ck = 0
_cw = {"ipb": 16, "convw": 32, "convb": 8, "dtbias": 8, "outb": 8, "lng": 8,
       "lnb": 8, "alog": 256, "w1s": 32, "w2s": 32, "h1s": 32, "h2s": 32,
       "w3s": 8, "bcb": 64}
for _n, _w in _cw.items():
    _offs[_n] = _ck
    _ck += _w
CK = _ck

DEBUG_TAPS = {}
FFN_INTERLEAVE = 3


def build_program():
    nc = bass.Bass("TRN2", target_bir_lowering=False)

    xT = nc.dram_tensor("xT", [D, T], F32, kind="ExternalInput").ap()
    consts_d = nc.dram_tensor("consts", [128, CK], F32, kind="ExternalInput").ap()
    win = {}
    for nm, shp in [("ipT", [D, 2 * D]), ("dtT", [D, D]), ("outT", [D, D]),
                    ("bcT", [D, 64]), ("w1T", [D, 4 * D]), ("w2T", [D, 4 * D]),
                    ("h1T", [4 * D, 4 * D]), ("h2T", [4 * D, 4 * D]),
                    ("w3T", [4 * D, D])]:
        win[nm] = nc.dram_tensor(nm, shp, F32, kind="ExternalInput").ap()
    wbf = {nm: nc.dram_tensor(nm + "b", list(win[nm].shape), BF16, kind="Internal").ap()
           for nm in win}
    oT = nc.dram_tensor("oT", [D, T], F32, kind="ExternalOutput").ap()
    dbg_t = {}
    for _dn in DEBUG_TAPS:
        dbg_t[_dn] = None
    btct = [nc.dram_tensor(f"btct{i}", [TC * 64], F32, kind="Internal").ap()
            for i in range(2)]
    statscr = [nc.dram_tensor(f"stsc{i}", [520], F32, kind="Internal").ap()
               for i in range(4)]
    lnmu_d = nc.dram_tensor("lnmu", [256], F32, kind="Internal").ap()
    lnrs_d = nc.dram_tensor("lnrs", [256], F32, kind="Internal").ap()

    es = ExitStack()
    with es:
        def sb(name, shape, dt=F32):
            return es.enter_context(nc.sbuf_tensor(name, shape, dt))

        consts = sb("consts_sb", [128, CK])
        Ap = sb("Ap_sb", [128, DN])
        ones_bf = sb("ones_bf", [128, 1], BF16)
        ones_row = sb("ones_row", [1, 128])
        cap10 = sb("cap10", [128, 1])
        ncap10 = sb("ncap10", [128, 1])
        c_eps = sb("c_eps", [128, 1])
        c_one = sb("c_one", [128, 1])
        X = sb("X", [128, QD * (TC + 1)])
        xnb = sb("xnb", [128, QD * (TC + 1)], BF16)
        xc = sb("xc", [128, QD * (TC + 3)])
        halo = sb("halo", [128, QD * 2])
        zb = sb("zb", [128, QD * TC])
        szh = sb("szh", [128, QD * TC], FP16)
        xs = sb("xs", [128, QD * TC], FP16)
        xsb = sb("xsb", [128, QD * TC], BF16)
        xsc = sb("xsc", [128, QD * TC], BF16)
        sw = sb("sw", [128, 8 * TC], FP16)
        sa_s = sb("sa_s", [128, DN])
        raw_s = sb("raw_s", [128, DN])
        acc_s = sa_s
        dtb_t = sb("dtb_t", [128, TC * QD], FP16)
        dxb_t = sb("dxb_t", [128, TC * QD], FP16)
        chist = sb("chist", [128, W * DN])
        amb = [sb(f"amb{i}", [128, W * DN], FP16) for i in range(3)]
        abar = [sb(f"abar{i}", [128, W * DN]) for i in range(3)]
        ub = [sb(f"ub{i}", [128, W * DN], FP16) for i in range(2)]
        brep = [sb(f"brep{i}", [128, W * NST]) for i in range(2)]
        crep = [sb(f"crep{i}", [128, W * NST]) for i in range(2)]
        hb = sb("hb", [128, 32 * TC], BF16)
        g1b = sb("g1b", [128, 32 * TC], BF16)
        x2 = sb("x2", [128, QD * TC])
        fb = sb("fb", [128, QD * TC], BF16)
        Fb = sb("Fb", [128, QD * TC])
        G = sb("G", [128, QD * (TC + 1)])
        stat = sb("stat", [128, 784])
        ysw = sb("ysw", [128, W * QD])
        bcsb = sb("bcsb", [128, 64])
        stat_cast = sb("stat_cast", [128, QD * (TC + 1)], BF16)
        slabm = [sb(f"slabm{i}", [128, QD * 128], BF16) for i in range(2)]
        slabf = [sb(f"slabf{i}", [128, 16 * 128], BF16) for i in range(4)]
        bcw = sb("bcw", [128, QD * 64], BF16)

        pb = [es.enter_context(nc.psum_tensor(f"pb{i}", [128, 512], F32))
              for i in range(8)]

        sems = {}
        for snm in ["v", "p", "s", "t", "st", "dtap",
                    "qm0", "qm1", "qf0", "qf1", "qf2", "qf3", "qx", "qbt0", "qbt1",
                    "wc0", "wc1", "wc2", "wc3", "wc4", "wc5", "wc6", "wc7",
                    "lnb0", "lnb1",
                    "qci0", "qci1", "qco0", "qco1", "qmisc",
                    "sscr", "srep0", "srep1", "sbr0", "sbr1", "scr0", "scr1"]:
            sems[snm] = [es.enter_context(nc.semaphore(snm)), 0]

        block = es.enter_context(nc.Block())

        prog = {k: [] for k in "QVPST"}
        lastw = {}

        def dep(key, cname, val):
            if val is None or val <= 0:
                return
            if lastw.get((key, cname), 0) >= val:
                return
            lastw[(key, cname)] = val
            sem = sems[cname][0]
            prog[key].append(lambda e, s=sem, v=val: e.wait_ge(s, v))

        def run(key, fn, inc=None, inck=1):
            if inc is None:
                prog[key].append(fn)
                return None
            sems[inc][1] += inck
            v = sems[inc][1]
            sem = sems[inc][0]

            def g(e, fn=fn, s=sem, k=inck):
                fn(e).then_inc(s, k)
            prog[key].append(g)
            return v

        def cur(cname):
            return sems[cname][1]

        bank = [{"fill": ("t", 0), "free": ("v", 0)} for _ in range(8)]

        def bank_consumed(bi, cname, val):
            bank[bi]["free"] = (cname, val)

        def mm_group(bi, pairs, out_ap, deps=()):
            fc, fv = bank[bi]["free"]
            dep("T", fc, fv)
            for cn, v in deps:
                dep("T", cn, v)
            n = len(pairs)
            tv = None
            for i, (lf, rf) in enumerate(pairs):
                a, b = (i == 0), (i == n - 1)
                fn = (lambda e, lf=lf, rf=rf, o=out_ap, a=a, b=b:
                      e.matmul(out=o, lhsT=lf(), rhs=rf(), start=a, stop=b))
                if b:
                    tv = run("T", fn, inc="t")
                else:
                    run("T", fn)
            bank[bi]["fill"] = ("t", tv)
            return tv

        def tap(name, c_now, src_ap, dtype, deplist):
            if DEBUG_TAPS.get(name) != c_now:
                return
            shp = list(src_ap.shape)
            h = nc.dram_tensor("dbg_" + name, shp, dtype, kind="ExternalOutput").ap()
            for cn, v in deplist:
                dep("Q", cn, v)
            run("Q", lambda e, h=h, s=src_ap: e.dma_start(out=h, in_=s),
                inc="dtap", inck=16)
            for k in "VPSTQ":
                dep(k, "dtap", cur("dtap"))

        def cv(name, i=None):
            o = _offs[name]
            if i is None:
                return consts[:, o:o + _cw[name]]
            return consts[:, o + i:o + i + 1]

        def qv(t_ap, ncol):
            return t_ap.rearrange("p (q j) -> p q j", j=ncol)

        slabm_free = [("t", 0), ("t", 0)]
        slabf_free = [("t", 0)] * 4
        ffn_rr = [0]
        w12_pend = deque()
        sw_v = [0, 0, 0, 0]

        def load_slab(buf_list, free_list, slot, src_ap, deps=()):
            fc, fv = free_list[slot]
            dep("Q", fc, fv)
            for cn, v in deps:
                dep("Q", cn, v)
            cls = (("qm0", "qm1") if buf_list is slabm
                   else ("qf0", "qf1", "qf2", "qf3"))
            ncols = src_ap.size() // 128
            v = run("Q", lambda e, b=buf_list[slot], s=src_ap, nn=ncols:
                    e.dma_start(out=b[:, 0:nn], in_=s), inc=cls[slot], inck=16)
            return (cls[slot], v)

        run("Q", lambda e: e.dma_start(out=consts[:], in_=consts_d),
            inc="qmisc", inck=16)
        ldc = cur("qmisc")
        dep("V", "qmisc", ldc)
        run("V", lambda e: e.memset(ones_bf[:], 1.0))
        run("V", lambda e: e.memset(ones_row[:], 1.0))
        run("V", lambda e: e.memset(cap10[:], 10.0))
        run("V", lambda e: e.memset(ncap10[:], -10.0))
        run("V", lambda e: e.memset(c_eps[:], EPS))
        run("V", lambda e: e.memset(c_one[:], 1.0))
        run("V", lambda e: e.memset(chist[:, (W - 1) * DN:W * DN], 0.0))
        run("V", lambda e: e.memset(halo[:], 0.0))
        run("V", lambda e: e.tensor_scalar(out=G[:, 0:DN], in0=cv("alog"),
                                           scalar1=2.0, scalar2=-5.0,
                                           op0=ALU.min, op1=ALU.max))
        v_init = run("V", lambda e: e.engine_nop(), inc="v")
        dep("S", "v", v_init)
        s_ap = run("S", lambda e: e.activation(out=Ap[:], in_=G[:, 0:DN],
                                               func=AF.Exp), inc="s")
        dep("P", "s", s_ap)
        dep("T", "v", v_init)

        cast_dma = ["bcT", "ipT", "dtT", "outT", "w1T", "w2T", "w3T", "h2T"]
        cast_ffn = [("h1T", 4 * D, 4 * D, "w2s")]
        ci = [hb[:, 0:4096].bitcast(F32), hb[:, 4096:8192].bitcast(F32)]
        co = [g1b[:, 0:4096].bitcast(BF16), g1b[:, 4096:8192].bitcast(BF16)]
        cast_cp = [("v", 0), ("v", 0)]
        cast_out = [("qco0", 0), ("qco1", 0)]
        cast_pi = [0]
        cast_done_ld = {}

        def emit_cast_tile(nm, src, dst, r0, c0, cc, scnm, is_last):
            pi = cast_pi[0]
            b = pi % 2
            qci = "qci0" if b == 0 else "qci1"
            qco = "qco0" if b == 0 else "qco1"
            cn, vv0 = cast_cp[b]
            dep("Q", cn, vv0)
            ldv = run("Q", lambda e, s=src, r0=r0, c0=c0, cc=cc, b=b:
                      e.dma_start(out=ci[b][:, 0:cc],
                                  in_=s[r0:r0 + 128, c0:c0 + cc]),
                      inc=qci, inck=16)
            key = "S" if (scnm is not None or pi % 2 == 0) else "V"
            dep(key, qci, ldv)
            cn2, vv2 = cast_out[b]
            dep(key, cn2, vv2)
            if scnm is not None:
                kq = r0 // 128
                cpv = run("S", lambda e, b=b, cc=cc, col=cv(scnm, kq):
                          e.activation(out=co[b][:, 0:cc], in_=ci[b][:, 0:cc],
                                       func=AF.Identity, scale=col), inc="s")
                cpc = "s"
            elif key == "S":
                cpv = run("S", lambda e, b=b, cc=cc:
                          e.copy(out=co[b][:, 0:cc], in_=ci[b][:, 0:cc]),
                          inc="s")
                cpc = "s"
            else:
                cpv = run("V", lambda e, b=b, cc=cc:
                          e.tensor_copy(out=co[b][:, 0:cc], in_=ci[b][:, 0:cc]),
                          inc="v")
                cpc = "v"
            cast_cp[b] = (cpc, cpv)
            dep("Q", cpc, cpv)
            ldo = run("Q", lambda e, d=dst, r0=r0, c0=c0, cc=cc, b=b:
                      e.dma_start(out=d[r0:r0 + 128, c0:c0 + cc],
                                  in_=co[b][:, 0:cc]), inc=qco, inck=16)
            cast_out[b] = (qco, ldo)
            cast_pi[0] += 1
            if is_last:
                cast_done_ld[nm] = [("qco0", cur("qco0")), ("qco1", cur("qco1"))]

        def tiles_of(nm, R, C, scnm):
            src, dst = win[nm], wbf[nm]
            tl = [(r0, c0, min(2048, C - c0))
                  for r0 in range(0, R, 128) for c0 in range(0, C, 2048)]
            return [(nm, src, dst, r0, c0, cc, scnm, i == len(tl) - 1)
                    for i, (r0, c0, cc) in enumerate(tl)]

        for i, nm in enumerate(cast_dma):
            wcl = f"wc{i}"
            v = run("P", lambda e, nm=nm: e.dma_start(
                out=wbf[nm].rearrange("a b -> (a b)"),
                in_=win[nm].rearrange("a b -> (a b)")), inc=wcl, inck=16)
            cast_done_ld[nm] = [(wcl, v)]

        cast_tasks = deque()
        for nm, R, C, scnm in cast_ffn:
            for args in tiles_of(nm, R, C, scnm):
                cast_tasks.append(lambda a=args: emit_cast_tile(*a))

        for cn_, v_ in cast_done_ld["bcT"]:
            dep("Q", cn_, v_)
        dep("Q", "qmisc", ldc)
        bcldv = run("Q", lambda e: e.dma_start(
            out=bcw[:], in_=wbf["bcT"].rearrange("(q p) j -> p q j", p=128)),
            inc="qmisc", inck=16)
        bcld_cls = "qmisc" 

        ln_ring = [0]
        scr_ring = [0]
        last_d1 = [0]
        last_vp = [0]
        last_tp = [0]
        scr_free = [("v", 0)] * 4
        rep_free = [("v", 0)] * 2

        ln_state = {}

        def emit_ln(src_fn, ncol, cast_buf, deps=()):
            for cn, v in deps:
                dep("S", cn, v)
            c1 = run("S", lambda e: e.copy(
                out=cast_buf[:, 0:QD * ncol].rearrange("p (q j) -> p q j", j=ncol),
                in_=src_fn()), inc="s")
            fc, fv = bank[5]["free"]
            dep("T", fc, fv)
            dep("T", "s", c1)
            t1 = None
            for q in range(QD):
                a, bl = (q == 0), (q == QD - 1)
                fn = (lambda e, q=q, ncol=ncol, cb=cast_buf, a=a, bl=bl:
                      e.matmul(out=pb[5][0:1, 0:ncol], lhsT=ones_bf[:],
                               rhs=cb[:, q * ncol:(q + 1) * ncol], start=a, stop=bl))
                t1 = run("T", fn, inc="t") if bl else run("T", fn)
            bank[5]["fill"] = ("t", t1)
            dep("S", "t", t1)
            c2 = run("S", lambda e: e.activation(
                out=cast_buf[:, 0:QD * ncol].rearrange("p (q j) -> p q j", j=ncol),
                in_=src_fn(), func=AF.Square), inc="s")
            fc, fv = bank[4]["free"]
            dep("T", fc, fv)
            dep("T", "s", c2)
            t2 = None
            for q in range(QD):
                a, bl = (q == 0), (q == QD - 1)
                fn = (lambda e, q=q, ncol=ncol, cb=cast_buf, a=a, bl=bl:
                      e.matmul(out=pb[4][0:1, 0:ncol], lhsT=ones_bf[:],
                               rhs=cb[:, q * ncol:(q + 1) * ncol], start=a, stop=bl))
                t2 = run("T", fn, inc="t") if bl else run("T", fn)
            bank[4]["fill"] = ("t", t2)
            dep("S", "t", t1)
            dep("S", "t", last_tp[0])
            s_mu = run("S", lambda e, ncol=ncol: e.activation(
                out=stat[0:1, 0:ncol], in_=pb[5][0:1, 0:ncol],
                func=AF.Identity, scale=1.0 / D), inc="s")
            bank_consumed(5, "s", s_mu)
            dep("S", "s", s_mu)
            s_m2 = run("S", lambda e, ncol=ncol: e.activation(
                out=stat[32:33, 0:ncol], in_=stat[0:1, 0:ncol], func=AF.Square),
                inc="s")
            ln_state["t2"] = t2
            ln_state["s_m2"] = s_m2
            ln_state["s_mu"] = s_mu
            ln_state["ncol"] = ncol

        def emit_ln_fin():
            ncol = ln_state["ncol"]
            dep("V", "t", ln_state["t2"])
            dep("V", "s", ln_state["s_m2"])
            v_var = run("V", lambda e, ncol=ncol: e.scalar_tensor_tensor(
                out=stat[64:65, 0:ncol], in0=pb[4][0:1, 0:ncol],
                scalar=1.0 / D, in1=stat[32:33, 0:ncol],
                op0=ALU.mult, op1=ALU.subtract), inc="v")
            bank_consumed(4, "v", v_var)
            dep("S", "v", v_var)
            s_l = run("S", lambda e, ncol=ncol: e.activation(
                out=stat[96:97, 0:ncol], in_=stat[64:65, 0:ncol], func=AF.Ln,
                bias=c_eps[0:1, :]), inc="s")
            dep("S", "s", s_l)
            s_r = run("S", lambda e, ncol=ncol: e.activation(
                out=stat[0:1, 272:272 + ncol], in_=stat[96:97, 0:ncol], func=AF.Exp,
                scale=-0.5), inc="s")
            fc, fv = bank[5]["free"]
            dep("T", fc, fv)
            dep("T", "s", s_r)
            tr_ = run("T", lambda e, ncol=ncol: e.matmul(
                out=pb[5][:, 0:ncol], lhsT=ones_row[:],
                rhs=stat[0:1, 272:272 + ncol], start=True, stop=True), inc="t")
            bank[5]["fill"] = ("t", tr_)
            fc, fv = bank[4]["free"]
            dep("T", fc, fv)
            dep("T", "s", ln_state["s_mu"])
            tp_ = run("T", lambda e, ncol=ncol: e.matmul(
                out=pb[4][:, 0:ncol], lhsT=ones_row[:],
                rhs=stat[0:1, 0:ncol], start=True, stop=True),
                inc="t")
            bank[4]["fill"] = ("t", tp_)
            last_tp[0] = tp_
            return tp_

        def rep_views(_, ncol):
            rstd = pb[5][:, 0:ncol].unsqueeze(1).broadcast_to([128, QD, ncol])
            p_ = pb[4][:, 0:ncol].unsqueeze(1).broadcast_to([128, QD, ncol])
            return rstd, p_

        HB = 136
        X_free_h = [("v", 0), ("v", 0)]
        xc_free = ("v", 0)
        xnb_free_h = [("t", 0), ("t", 0)]
        ygb_wfree = ("t", 0)
        fb_wfree = ("t", 0)
        out_dma_prev = ("st", 0)
        hb_ready = ("v", 0)
        zb_free = ("v", 0)
        x2b_free_h = [("v", 0), ("v", 0)]
        G_s = [0]
        G_v = [0]
        win_vfree = {}
        gw = [0]
        vdt_map = {}
        szh_map = {}
        btct_map = {}
        gate_map = {}
        ipb_cols = [cv("ipb", i) for i in range(16)]
        side = deque()
        ffn_tasks = deque()
        cur_ffn_chunk = [0]
        x2scr = [nc.dram_tensor(f"x2scr{i}", [128, QD * TC], F32, kind="Internal").ap()
                 for i in range(2)]
        x2_reload_map = {}

        def vh(fn):
            fn.vheavy = True
            return fn

        def tslice(buf, ncol_layout, t0, t1):
            return buf[:].rearrange("p (q j) -> p q j", j=ncol_layout)[:, :, t0:t1]

        ffn_ok = [False]

        def w1w2_group(cc, olist):
            def f():
                nonlocal hb_ready, fb_wfree
                cur_ffn_chunk[0] = cc
                t1 = t2 = 0

                def flush_pend():
                    if not w12_pend:
                        return 0
                    po, psv = w12_pend.popleft()
                    pso = po % 4
                    dep("V", "s", psv)
                    hv = run("V", lambda e, po=po, pso=pso: e.tensor_tensor(
                        out=hb[:, po * TC:(po + 1) * TC],
                        in0=sw[:, pso * TC:(pso + 1) * TC],
                        in1=sw[:, (4 + pso) * TC:(5 + pso) * TC],
                        op=ALU.mult), inc="v")
                    sw_v[pso] = hv
                    return hv

                if cc == 0:
                    dep("V", "qci0", cur("qci0"))
                    dep("V", "qci1", cur("qci1"))
                dep("T", "v", cur("v"))
                for o in olist:
                    so = o % 2
                    ld1 = load_slab(slabm, slabm_free, so,
                                    wbf["w1T"].rearrange("(k p) j -> p k j", p=128)
                                    [:, :, o * 128:(o + 1) * 128],
                                    deps=cast_done_ld["w1T"])
                    sl2 = ffn_rr[0] % 4
                    ffn_rr[0] += 1
                    ld2 = load_slab(slabf, slabf_free, sl2,
                                    wbf["w2T"].rearrange("(k p) j -> p k j", p=128)
                                    [:, :, o * 128:(o + 1) * 128],
                                    deps=cast_done_ld["w2T"])
                    pairs1 = [((lambda so=so, k=k: slabm[so][:, k * 128:(k + 1) * 128]),
                               (lambda k=k: fb[:, k * TC:(k + 1) * TC]))
                              for k in range(QD)]
                    t1 = mm_group(0, pairs1, pb[0][:, 0:TC], deps=[ld1])
                    slabm_free[so] = ("t", t1)
                    pairs2 = [((lambda sl2=sl2, k=k:
                                slabf[sl2][:, k * 128:(k + 1) * 128]),
                               (lambda k=k: fb[:, k * TC:(k + 1) * TC]))
                              for k in range(QD)]
                    t2 = mm_group(1, pairs2, pb[1][:, 0:TC], deps=[ld2])
                    slabf_free[sl2] = ("t", t2)
                    dep("S", "t", t1)
                    dep("S", "v", sw_v[o % 4])
                    run("S", lambda e, o=o: e.activation(
                        out=sw[:, (o % 4) * TC:(o % 4 + 1) * TC],
                        in_=pb[0][:, 0:TC], func=AF.Silu, scale=cv("w1s", o)),
                        inc="s")
                    dep("S", "t", t2)
                    sv = run("S", lambda e, o=o: e.activation(
                        out=sw[:, (4 + o % 4) * TC:(5 + o % 4) * TC],
                        in_=pb[1][:, 0:TC], func=AF.Identity), inc="s")
                    bank_consumed(0, "s", sv)
                    bank_consumed(1, "s", sv)
                    if len(w12_pend) == 4:
                        flush_pend()
                    w12_pend.append((o, sv))
                if olist[-1] == 31:
                    hv = 0
                    while w12_pend:
                        hv = flush_pend()
                    hb_ready = ("v", hv)
                    fb_wfree = ("t", max(t1, t2))
                    ffn_ok[0] = True
            return f

        HB_BANKS = (2, 3, 7)

        def mk_ffn_tasks(cc, cell):
            tasks = deque()

            def h_round(o, wnm, snm, cellkey):
                def f():
                    bi = HB_BANKS[o % 3]
                    fc, fv = bank[bi]["free"]
                    slots, lds = [], []
                    for half in range(2):
                        sl = ffn_rr[0] % 4
                        ffn_rr[0] += 1
                        slots.append(sl)
                        lds.append(load_slab(
                            slabf, slabf_free, sl,
                            wbf[wnm].rearrange("(k p) j -> p k j", p=128)
                            [:, half * 16:(half + 1) * 16, o * 128:(o + 1) * 128],
                            deps=cast_done_ld[wnm]))
                    dep("T", fc, fv)
                    if wnm == "h1T":
                        dep("T", "v", hb_ready[1])
                    else:
                        dep("T", "s", cell["g1_s"])
                    tv = None
                    rhs_buf = hb if wnm == "h1T" else g1b
                    for half in range(2):
                        dep("T", lds[half][0], lds[half][1])
                        for k in range(16):
                            kk = half * 16 + k
                            a = (kk == 0)
                            bl = (kk == 31)
                            fn = (lambda e, kk=kk, sl=slots[half], bi=bi, a=a, bl=bl,
                                  rb=rhs_buf, k=k:
                                  e.matmul(out=pb[bi][:, 0:TC],
                                           lhsT=slabf[sl][:, k * 128:(k + 1) * 128],
                                           rhs=rb[:, kk * TC:(kk + 1) * TC],
                                           start=a, stop=bl))
                            tv = run("T", fn, inc="t") if bl else run("T", fn)
                    slabf_free[slots[0]] = ("t", tv)
                    slabf_free[slots[1]] = ("t", tv)
                    bank[bi]["fill"] = ("t", tv)
                    cell[cellkey] = max(cell[cellkey], tv)
                    dep("S", "t", tv)
                    if cc == 0 and wnm == "h1T" and o == 0:
                        dep("S", "qco0", cur("qco0"))
                        dep("S", "qco1", cur("qco1"))
                    if wnm == "h2T" and o == 0:
                        dep("S", "t", cell["h1_t"])
                    dst = g1b if wnm == "h1T" else hb
                    sv = run("S", lambda e, o=o, bi=bi, snm=snm, dst=dst:
                             e.activation(
                        out=dst[:, o * TC:(o + 1) * TC], in_=pb[bi][:, 0:TC],
                        func=AF.Gelu, scale=cv(snm, o)), inc="s")
                    bank_consumed(bi, "s", sv)
                    if o == 31:
                        cell["g1_s" if wnm == "h1T" else "g2_s"] = sv
                return f

            def w3_round(o):
                def f():
                    bi = HB_BANKS[o % 3]
                    fc, fv = bank[bi]["free"]
                    slots, lds = [], []
                    for half in range(2):
                        sl = ffn_rr[0] % 4
                        ffn_rr[0] += 1
                        slots.append(sl)
                        lds.append(load_slab(
                            slabf, slabf_free, sl,
                            wbf["w3T"].rearrange("(k p) j -> p k j", p=128)
                            [:, half * 16:(half + 1) * 16, o * 128:(o + 1) * 128],
                            deps=cast_done_ld["w3T"]))
                    dep("T", fc, fv)
                    dep("T", "s", cell["g2_s"])
                    tv = None
                    for half in range(2):
                        dep("T", lds[half][0], lds[half][1])
                        for k in range(16):
                            kk = half * 16 + k
                            a = (kk == 0)
                            bl = (kk == 31)
                            fn = (lambda e, kk=kk, sl=slots[half], bi=bi, a=a, bl=bl,
                                  k=k:
                                  e.matmul(out=pb[bi][:, 0:TC],
                                           lhsT=slabf[sl][:, k * 128:(k + 1) * 128],
                                           rhs=hb[:, kk * TC:(kk + 1) * TC],
                                           start=a, stop=bl))
                            tv = run("T", fn, inc="t") if bl else run("T", fn)
                    slabf_free[slots[0]] = ("t", tv)
                    slabf_free[slots[1]] = ("t", tv)
                    bank[bi]["fill"] = ("t", tv)
                    dep("S", "t", tv)
                    if o == 0:
                        cn, vv = out_dma_prev
                        dep("S", cn, vv)
                    sv = run("S", lambda e, o=o, bi=bi: e.activation(
                        out=Fb[:, o * TC:(o + 1) * TC], in_=pb[bi][:, 0:TC],
                        func=AF.Identity, scale=cv("w3s", o)), inc="s")
                    bank_consumed(bi, "s", sv)
                return f

            for o in range(32):
                tasks.append(h_round(o, "h1T", "h1s", "h1_t"))
            for o in range(32):
                tasks.append(h_round(o, "h2T", "h2s", "h2_t"))
            for o in range(QD):
                tasks.append(w3_round(o))
            return tasks

        def ln_apply_to_G(src_fn, ncol, t0, t1, out2=None):
            n = t1 - t0
            rstd = pb[5][:, 0:n].unsqueeze(1).broadcast_to([128, QD, n])
            mu_b = pb[4][:, 0:n].unsqueeze(1).broadcast_to([128, QD, n])
            dep("V", "t", last_tp[0])
            dep("V", "s", G_s[0])
            run("V", lambda e: e.tensor_tensor(
                out=tslice(G, TC + 1, t0, t1), in0=src_fn(), in1=mu_b,
                op=ALU.subtract))
            dst = out2 if out2 is not None else (lambda: tslice(G, TC + 1, t0, t1))
            vv = run("V", lambda e: e.tensor_tensor(
                out=dst(), in0=tslice(G, TC + 1, t0, t1),
                in1=rstd, op=ALU.mult), inc="v")
            G_v[0] = vv
            bank_consumed(5, "v", vv)
            bank_consumed(4, "v", vv)
            return vv

        def post_half_closures(c, h):
            t0, t1 = (0, HB) if h == 0 else (HB, TC)
            n = t1 - t0
            cls = []

            def c_outproj_a(opart):
                def f():
                    nonlocal ygb_wfree
                    dep_v = gate_map[(c, h)]
                    op_t = 0
                    for o in opart:
                        so = o % 2
                        ldv = load_slab(slabm, slabm_free, so,
                                        wbf["outT"].rearrange("(k p) j -> p k j", p=128)
                                        [:, :, o * 128:(o + 1) * 128],
                                        deps=cast_done_ld["outT"])
                        bi = o % 2
                        pairs = [((lambda so=so, k=k: slabm[so][:, k * 128:(k + 1) * 128]),
                                  (lambda k=k, t0=t0, t1=t1:
                                   xsb[:, k * TC + t0:k * TC + t1]))
                                 for k in range(QD)]
                        mmdeps = [ldv, ("v", dep_v)]
                        if o == 0 and h == 0:
                            mmdeps.append(x2b_free_h[0])
                        if o == 0 and h == 1:
                            mmdeps.append(x2b_free_h[1])
                        tv = mm_group(bi, pairs, pb[bi][:, 0:n], deps=mmdeps)
                        slabm_free[so] = ("t", tv)
                        op_t = tv
                        dep("S", "t", tv)
                        cn, vv = x2b_free_h[h]
                        dep("S", cn, vv)
                        sv = run("S", lambda e, o=o, bi=bi, t0=t0, t1=t1, n=n:
                                 e.activation(
                            out=x2[:, o * TC + t0:o * TC + t1], in_=pb[bi][:, 0:n],
                            func=AF.Identity, bias=cv("outb", o)), inc="s")
                        bank_consumed(bi, "s", sv)
                    if h == 1 and opart[-1] == 7:
                        ygb_wfree = ("t", op_t)
                return f
            cls.append(c_outproj_a([0, 1, 2, 3]))
            cls.append(c_outproj_a([4, 5, 6, 7]))

            def c_mr():
                def f():
                    dep("V", "s", cur("s"))
                    run("V", lambda e: e.tensor_tensor(
                        out=tslice(x2, TC, t0, t1), in0=tslice(x2, TC, t0, t1),
                        in1=tslice(xnb, TC + 1, t0, t1), op=ALU.add), inc="v")
                return f
            cls.append(vh(c_mr()))

            ln_bc = [0]

            def c_stats(src_fn, extra=()):
                def f():
                    emit_ln(src_fn, n, stat_cast,
                            deps=list(extra) + [("v", cur("v"))])
                return f

            def c_mixapply():
                def f():
                    emit_ln_fin()
                    ln_apply_to_G(lambda: tslice(x2, TC, t0, t1), n, t0, t1)
                    dep("V", "s", G_s[0])
                    vv = None
                    for q in range(QD):
                        vv = run("V", lambda e, q=q: e.scalar_tensor_tensor(
                            out=x2[:, q * TC + t0:q * TC + t1],
                            in0=G[:, q * (TC + 1) + t0:q * (TC + 1) + t1],
                            scalar=cv("lng", q),
                            in1=cv("lnb", q).broadcast_to([128, n]),
                            op0=ALU.mult, op1=ALU.add), inc="v")
                    G_v[0] = vv
                    bank_consumed(5, "v", vv)
                    bank_consumed(4, "v", vv)
                return f

            def c_ln2apply():
                def f():
                    nonlocal X_free_h
                    emit_ln_fin()
                    ln_apply_to_G(lambda: tslice(x2, TC, t0, t1), n, t0, t1)
                    vv = run("V", lambda e: e.tensor_tensor(
                        out=tslice(x2, TC, t0, t1), in0=tslice(G, TC + 1, t0, t1),
                        in1=tslice(X, TC + 1, t0, t1), op=ALU.add), inc="v")
                    G_v[0] = vv
                    X_free_h[h] = ("v", vv)
                    bank_consumed(5, "v", vv)
                    bank_consumed(4, "v", vv)
                return f

            def c_ln3apply():
                def f():
                    nonlocal x2b_free_h
                    emit_ln_fin()
                    cn, vv0 = fb_wfree
                    dep("V", cn, vv0)
                    vv = ln_apply_to_G(lambda: tslice(x2, TC, t0, t1), n, t0, t1,
                                       out2=lambda: tslice(fb, TC, t0, t1))
                    bank_consumed(5, "v", vv)
                    bank_consumed(4, "v", vv)
                    clsn = "qci0" if c % 2 == 0 else "qci1"
                    dep("Q", clsn, cur(clsn))
                    dep("Q", "v", vv)
                    dep("Q", "s", cur("s"))
                    sv = run("Q", lambda e, c=c, t0=t0, t1=t1: e.dma_start(
                        out=x2scr[c % 2].rearrange("p (q j) -> p q j", j=TC)
                        [:, :, t0:t1],
                        in_=tslice(x2, TC, t0, t1)), inc=clsn, inck=16)
                    x2b_free_h[h] = (clsn, sv)
                    x2_reload_map.setdefault(c, []).append((clsn, sv))
                return f

            cls.append(c_stats(lambda: tslice(x2, TC, t0, t1)))
            cls.append(vh(c_mixapply()))
            cls.append(c_stats(lambda: tslice(x2, TC, t0, t1)))
            cls.append(vh(c_ln2apply()))
            cls.append(c_stats(lambda: tslice(x2, TC, t0, t1)))
            cls.append(vh(c_ln3apply()))
            return cls

        def pre_half_closures(cn, h):
            t0, t1 = (0, HB) if h == 0 else (HB, TC + 1)
            n = t1 - t0
            last = (cn == NCH - 1)
            cv0, cv1 = (0, 133) if h == 0 else (133, TC)
            cls = []

            ln_bc1 = [0]

            def c_xload():
                def f():
                    cnf, vvf = X_free_h[h]
                    dep("Q", cnf, vvf)
                    qcl = "qx"
                    dep("Q", qcl, cur(qcl))
                    tload = t1 if not (last and h == 1) else TC
                    xl = run("Q", lambda e, cn=cn, t0=t0, tload=tload: e.dma_start(
                        out=tslice(X, TC + 1, t0, tload),
                        in_=xT.rearrange("(q p) t -> p q t", p=128)
                        [:, :, cn * TC + t0:cn * TC + tload]), inc=qcl, inck=16)
                    if last and h == 1:
                        dep("V", qcl, xl)
                        xl2 = run("V", lambda e: e.memset(
                            tslice(X, TC + 1, TC, TC + 1), 0.0), inc="v")
                        emit_ln(lambda: tslice(X, TC + 1, t0, t1), n,
                                stat_cast, deps=[("v", xl2)])
                    else:
                        emit_ln(lambda: tslice(X, TC + 1, t0, t1), n,
                                stat_cast, deps=[(qcl, xl)])
                return f
            cls.append(c_xload())

            def c_ln1apply():
                def f():
                    nonlocal xnb_free_h
                    emit_ln_fin()
                    cnf, vvf = xnb_free_h[h]
                    dep("V", cnf, vvf)
                    vv = ln_apply_to_G(lambda: tslice(X, TC + 1, t0, t1), n, t0, t1,
                                       out2=lambda: tslice(xnb, TC + 1, t0, t1))
                    bank_consumed(5, "v", vv)
                    bank_consumed(4, "v", vv)
                    xnb_v[h] = vv
                return f
            cls.append(vh(c_ln1apply()))

            ipev = [0] * 8
            sp_sl = [0]
            xnb_v = [0, 0]

            def c_inproj(opart):
                def f():
                    s_x = cur("s")
                    v_x = xnb_v[h]
                    for o in opart:
                        so = o % 2
                        ldv = load_slab(slabm, slabm_free, so,
                                        wbf["ipT"].rearrange("(k p) j -> p k j", p=128)
                                        [:, :, o * 128:(o + 1) * 128],
                                        deps=cast_done_ld["ipT"])
                        bi = o % 2
                        pairs = [((lambda so=so, k=k: slabm[so][:, k * 128:(k + 1) * 128]),
                                  (lambda k=k, t0=t0, t1=t1:
                                   xnb[:, k * (TC + 1) + t0:k * (TC + 1) + t1]))
                                 for k in range(QD)]
                        mmdeps = [ldv, ("s", s_x), ("v", v_x)]
                        if o == opart[0]:
                            mmdeps.append(xc_free if o < 8 else zb_free)
                        tv = mm_group(bi, pairs, pb[bi][:, 0:n], deps=mmdeps)
                        slabm_free[so] = ("t", tv)
                        dep("S", "t", tv)
                        if o < 8:
                            cnf, vvf = xc_free
                            dep("S", cnf, vvf)
                            sv = run("S", lambda e, o=o, bi=bi, t0=t0, n=n:
                                     e.activation(
                                out=xc[:, o * (TC + 3) + 2 + t0:
                                       o * (TC + 3) + 2 + t0 + n],
                                in_=pb[bi][:, 0:n], func=AF.Identity,
                                bias=ipb_cols[o]), inc="s")
                            ipev[o] = sv
                        else:
                            cnf, vvf = zb_free
                            dep("S", cnf, vvf)
                            nz = min(t1, TC) - t0
                            sv = run("S", lambda e, o=o, bi=bi, t0=t0, nz=nz:
                                     e.activation(
                                out=zb[:, (o - 8) * TC + t0:(o - 8) * TC + t0 + nz],
                                in_=pb[bi][:, 0:nz], func=AF.Identity,
                                bias=ipb_cols[o]), inc="s")
                        bank_consumed(bi, "s", sv)
                    if opart[-1] == 15:
                        xnb_free_h[h] = ("t", cur("t"))
                return f
            cls.append(c_inproj(list(range(0, 8))))
            cls.append(c_inproj(list(range(8, 16))))

            def c_conv():
                def f():
                    nonlocal xc_free
                    if h == 0:
                        cnf, vvf = xc_free
                        dep("S", cnf, vvf)
                        sh = run("S", lambda e: e.copy(
                            out=xc[:].rearrange("p (q j) -> p q j", j=TC + 3)[:, :, 0:2],
                            in_=halo[:].rearrange("p (q j) -> p q j", j=2)), inc="s")
                        dep("V", "s", sh)
                    if last and h == 1:
                        dep("V", "s", ipev[7])
                        run("V", lambda e: e.memset(
                            xc[:].rearrange("p (q j) -> p q j", j=TC + 3)
                            [:, :, TC + 2:TC + 3], 0.0), inc="v")
                    nn = cv1 - cv0
                    dep("V", "s", G_s[0])
                    for q in range(QD):
                        dep("V", "s", ipev[q])
                        base = q * (TC + 3)
                        for k in range(4):
                            src = xc[:, base + cv0 + k:base + cv0 + k + nn]
                            wcol = cv("convw", q * 4 + k)
                            if k == 0:
                                run("V", lambda e, src=src, wcol=wcol, q=q, nn=nn:
                                    e.scalar_tensor_tensor(
                                        out=sa_s[:, 0:nn], in0=src, scalar=wcol,
                                        in1=cv("convb", q).broadcast_to([128, nn]),
                                        op0=ALU.mult, op1=ALU.add))
                            elif k < 3:
                                srcacc = sa_s[:, 0:nn] if k == 1 else raw_s[:, 0:nn]
                                dst = raw_s[:, 0:nn] if k == 1 else sa_s[:, 0:nn]
                                run("V", lambda e, src=src, wcol=wcol, i1=srcacc, d=dst:
                                    e.scalar_tensor_tensor(out=d, in0=src, scalar=wcol,
                                                           in1=i1, op0=ALU.mult,
                                                           op1=ALU.add))
                            else:
                                run("V", lambda e, src=src, wcol=wcol, q=q, nn=nn:
                                    e.scalar_tensor_tensor(
                                        out=G[:, q * (TC + 1) + cv0:
                                              q * (TC + 1) + cv0 + nn],
                                        in0=src, scalar=wcol, in1=sa_s[:, 0:nn],
                                        op0=ALU.mult, op1=ALU.add))
                    if h == 1:
                        vv = run("V", lambda e: e.tensor_copy(
                            out=halo[:].rearrange("p (q j) -> p q j", j=2),
                            in_=xc[:].rearrange("p (q j) -> p q j", j=TC + 3)
                            [:, :, TC:TC + 2]), inc="v")
                        xc_free = ("v", vv)
                        G_v[0] = vv
                    else:
                        vv = run("V", lambda e: e.engine_nop(), inc="v")
                        G_v[0] = vv
                    dep("S", "v", vv)
                    s_xs = run("S", lambda e, cv0=cv0, nn=nn: e.activation(
                        out=tslice(xs, TC, cv0, cv0 + nn),
                        in_=tslice(G[:, 0:QD * (TC + 1)], TC + 1, cv0, cv0 + nn),
                        func=AF.Silu), inc="s")
                    G_s[0] = s_xs
                    z1 = min(t1, TC)
                    s_sz = run("S", lambda e, t0=t0, z1=z1: e.activation(
                        out=tslice(szh, TC, t0, z1), in_=tslice(zb, TC, t0, z1),
                        func=AF.Silu), inc="s")
                    szh_map[(cn, h)] = s_sz
                return f
            cls.append(vh(c_conv()))

            def c_bc():
                def f():
                    nn = cv1 - cv0
                    s_c = run("S", lambda e, cv0=cv0, nn=nn: e.copy(
                        out=tslice(xsc, TC, cv0, cv0 + nn),
                        in_=tslice(xs, TC, cv0, cv0 + nn)), inc="s")
                    blk = h
                    fc, fv = bank[6]["free"]
                    dep("T", fc, fv)
                    dep("T", "s", s_c)
                    dep("T", "qmisc", bcldv)
                    tv = None
                    for k in range(QD):
                        a, bl = (k == 0), (k == QD - 1)
                        fn = (lambda e, blk=blk, k=k, a=a, bl=bl:
                              e.matmul(out=pb[6][:, 0:64],
                                       lhsT=xsc[:, k * TC + blk * 128:
                                                k * TC + blk * 128 + 128],
                                       rhs=bcw[:, k * 64:(k + 1) * 64],
                                       start=a, stop=bl))
                        tv = run("T", fn, inc="t") if bl else run("T", fn)
                    bank[6]["fill"] = ("t", tv)
                    dep("V", "t", tv)
                    v_bc = run("V", lambda e: e.tensor_tensor(
                        out=bcsb[:].rearrange("p (a b) -> p a b", b=64)[:, 0:1, :],
                        in0=pb[6][:, 0:64].unsqueeze(1),
                        in1=cv("bcb").unsqueeze(1).broadcast_to([128, 1, 64]),
                        op=ALU.add), inc="v")
                    bank_consumed(6, "v", v_bc)
                    dep("Q", "v", v_bc)
                    qbt_cls = "qbt0" if cn % 2 == 0 else "qbt1"
                    dep("Q", qbt_cls, cur(qbt_cls))
                    q_bt = run("Q", lambda e, cn=cn, blk=blk: e.dma_start(
                        out=btct[cn % 2].rearrange("(b p j) -> p b j", p=128, j=64)
                        [:, blk:blk + 1, :],
                        in_=bcsb[:].rearrange("p (a b) -> p a b", b=64)[:, 0:1, :]),
                        inc=qbt_cls, inck=16)
                    btct_map[(cn, h)] = (qbt_cls, q_bt)
                return f
            cls.append(c_bc())

            def c_dt():
                def f():
                    b0 = h * 128
                    for o in range(QD):
                        so = o % 2
                        ldv = load_slab(slabm, slabm_free, so,
                                        wbf["dtT"].rearrange("(k p) j -> p k j", p=128)
                                        [:, :, o * 128:(o + 1) * 128],
                                        deps=cast_done_ld["dtT"])
                        bi = o % 2
                        pairs = [((lambda so=so, k=k: slabm[so][:, k * 128:(k + 1) * 128]),
                                  (lambda k=k, b0=b0: xnb[:, k * (TC + 1) + b0:
                                                          k * (TC + 1) + b0 + 128]))
                                 for k in range(QD)]
                        tv = mm_group(bi, pairs, pb[bi][:, 0:128], deps=[ldv])
                        slabm_free[so] = ("t", tv)
                        dep("S", "t", tv)
                        sv = run("S", lambda e, o=o, bi=bi, b0=b0: e.activation(
                            out=zb[:, o * TC + b0:o * TC + b0 + 128],
                            in_=pb[bi][:, 0:128], func=AF.Identity,
                            bias=cv("dtbias", o)), inc="s")
                        bank_consumed(bi, "s", sv)
                    if h == 1:
                        xnb_free_h[0] = ("t", cur("t"))
                return f
            cls.append(c_dt())

            def c_softplus():
                def f():
                    nonlocal zb_free
                    b0 = h * 128
                    cn_, vv_ = out_dma_prev
                    dep("S", cn_, vv_)
                    dep("S", "v", G_v[0])
                    s_ab = run("S", lambda e, b0=b0: e.activation(
                        out=tslice(G[:, 0:QD * (TC + 1)], TC + 1, b0, b0 + 128),
                        in_=tslice(zb, TC, b0, b0 + 128), func=AF.Abs), inc="s")
                    dep("S", "s", s_ab)
                    s_ex = run("S", lambda e, b0=b0: e.activation(
                        out=dxb_t[:, b0 * QD:(b0 + 128) * QD],
                        in_=tslice(G[:, 0:QD * (TC + 1)], TC + 1, b0, b0 + 128),
                        func=AF.Exp, scale=-1.0), inc="s")
                    dep("S", "s", s_ex)
                    s_l = run("S", lambda e, b0=b0: e.activation(
                        out=tslice(G[:, 0:QD * (TC + 1)], TC + 1, b0, b0 + 128),
                        in_=dxb_t[:, b0 * QD:(b0 + 128) * QD], func=AF.Ln,
                        bias=c_one[:]), inc="s")
                    G_s[0] = s_l
                    sp_sl[0] = s_l
                return f

            def c_softplus_v():
                def f():
                    nonlocal zb_free
                    b0 = h * 128
                    s_l = sp_sl[0]
                    dep("V", "s", s_l)
                    run("V", lambda e, b0=b0: e.scalar_tensor_tensor(
                        out=tslice(G[:, 0:QD * (TC + 1)], TC + 1, b0, b0 + 128),
                        in0=tslice(zb, TC, b0, b0 + 128), scalar=0.0, op0=ALU.max,
                        in1=tslice(G[:, 0:QD * (TC + 1)], TC + 1, b0, b0 + 128),
                        op1=ALU.add))
                    vz = run("V", lambda e, b0=b0: e.tensor_scalar(
                        out=dtb_t[:, b0 * QD:(b0 + 128) * QD].rearrange(
                            "p (t q) -> p t q", q=QD).rearrange("p t q -> p q t"),
                        in0=tslice(G[:, 0:QD * (TC + 1)], TC + 1, b0, b0 + 128),
                        scalar1=0.2, scalar2=1e-3, op0=ALU.min, op1=ALU.max), inc="v")
                    zb_free = ("v", vz)
                    vd = run("V", lambda e, b0=b0: e.tensor_tensor(
                        out=dxb_t[:, b0 * QD:(b0 + 128) * QD].rearrange(
                            "p (t q) -> p t q", q=QD).rearrange("p t q -> p q t"),
                        in0=dtb_t[:, b0 * QD:(b0 + 128) * QD].rearrange(
                            "p (t q) -> p t q", q=QD).rearrange("p t q -> p q t"),
                        in1=tslice(xs, TC, b0, b0 + 128), op=ALU.mult), inc="v")
                    G_v[0] = vd
                    vdt_map[(cn, h)] = vd
                return f
            cls.append(c_softplus())
            cls.append(vh(c_softplus_v()))
            return cls

        def emit_am2(cc, w):
            g = cc * NWIN + w
            b3 = g % 3
            cnf, vvf = win_vfree.get(g - 3, ("v", 0))
            dep("P", cnf, vvf)
            dep("P", "v", vdt_map[(cc, w // 16)])
            pa = run("P", lambda e, w=w, b3=b3: e.tensor_tensor(
                out=amb[b3][:].rearrange("p (t q n) -> p t q n", q=QD, n=NST),
                in0=dtb_t[:, w * W * QD:(w + 1) * W * QD].rearrange(
                    "p (t q) -> p t q", q=QD).unsqueeze(3).broadcast_to(
                    [128, W, QD, NST]),
                in1=Ap[:].rearrange("p (q n) -> p q n", n=NST).unsqueeze(1)
                .broadcast_to([128, W, QD, NST]), op=ALU.mult), inc="p")
            dep("S", cnf, vvf)
            dep("S", "p", pa)
            se = run("S", lambda e, b3=b3: e.activation(
                out=abar[b3][:], in_=amb[b3][:], func=AF.Exp, scale=-1.0),
                inc="s")
            return se

        def emit_u2(cc, w):
            g = cc * NWIN + w
            b2 = g % 2
            cnf, vvf = win_vfree.get(g - 2, ("v", 0))
            bcls, bval = btct_map[(cc, w // 16)]
            dep("S", cnf, vvf)
            dep("S", bcls, bval)
            sbr = ("sbr0", "sbr1")[b2]
            scr = ("scr0", "scr1")[b2]
            db = run("S", lambda e, w=w, b2=b2, cc=cc: e.dma_start(
                out=brep[b2][:].rearrange("p (t j) -> p t j", j=NST),
                in_=btct[cc % 2].rearrange("(t j) -> t j", j=64)
                [w * W:(w + 1) * W, 0:NST].unsqueeze(0)
                .broadcast_to([128, W, NST])), inc=sbr, inck=16)
            dc = run("S", lambda e, w=w, b2=b2, cc=cc: e.dma_start(
                out=crep[b2][:].rearrange("p (t j) -> p t j", j=NST),
                in_=btct[cc % 2].rearrange("(t j) -> t j", j=64)
                [w * W:(w + 1) * W, NST:64].unsqueeze(0)
                .broadcast_to([128, W, NST])), inc=scr, inck=16)
            dep("P", cnf, vvf)
            dep("P", "v", vdt_map[(cc, w // 16)])
            dep("P", sbr, db)
            pu = run("P", lambda e, w=w, b2=b2: e.tensor_tensor(
                out=ub[b2][:].rearrange("p (t q n) -> p t q n", q=QD, n=NST),
                in0=dxb_t[:, w * W * QD:(w + 1) * W * QD].rearrange(
                    "p (t q) -> p t q", q=QD).unsqueeze(3).broadcast_to(
                    [128, W, QD, NST]),
                in1=brep[b2][:].rearrange("p (t n) -> p t n", n=NST)
                .unsqueeze(2).broadcast_to([128, W, QD, NST]),
                op=ALU.mult), inc="p")
            return pu, dc, scr

        amq = []
        uq = []

        for c in range(NCH + 1):
            cm1 = c - 1
            have_cur = c < NCH
            have_prev = cm1 >= 0

            cell = {"g1_s": 0, "g2_s": 0, "h1_t": 0, "h2_t": 0}
            if have_prev:
                ffn_tasks = mk_ffn_tasks(cm1, cell)
                ffn_ok[0] = False
            else:
                ffn_tasks = cast_tasks
                ffn_ok[0] = True

            if c == 0:
                for f in pre_half_closures(0, 0):
                    f()
                for f in pre_half_closures(0, 1):
                    f()

            if have_cur:
                if c == 0:
                    amq = [emit_am2(0, 0), emit_am2(0, 1)]
                    uq = [emit_u2(0, 0), emit_u2(0, 1)]
                for w in range(NWIN):
                    g = gw[0]
                    b3 = g % 3
                    b2 = g % 2
                    se = amq.pop(0)
                    pu, dc, scr = uq.pop(0)
                    dep("V", "s", se)
                    dep("V", "p", pu)
                    v_mid = None
                    for tr in range(W):
                        prev = (tr - 1) % W
                        run("V", lambda e, prev=prev, tr=tr, b3=b3: e.tensor_tensor(
                            out=sa_s[:], in0=chist[:, prev * DN:(prev + 1) * DN],
                            in1=abar[b3][:, tr * DN:(tr + 1) * DN], op=ALU.mult))
                        run("V", lambda e, tr=tr, b2=b2: e.tensor_tensor(
                            out=raw_s[:], in0=sa_s[:],
                            in1=ub[b2][:, tr * DN:(tr + 1) * DN], op=ALU.add))
                        v_mid = run("V", lambda e, tr=tr: e.scalar_tensor_tensor(
                            out=chist[:, tr * DN:(tr + 1) * DN], in0=raw_s[:],
                            scalar=10.0, in1=ncap10[:].broadcast_to([128, DN]),
                            op0=ALU.min, op1=ALU.max),
                            inc="v" if tr == W - 1 else None)
                    dep("V", scr, dc)
                    run("V", lambda e, b3=b3, b2=b2: e.tensor_tensor(
                        out=amb[b3][:].rearrange("p (t q n) -> p t q n", q=QD, n=NST),
                        in0=chist[:].rearrange("p (t q n) -> p t q n", q=QD, n=NST),
                        in1=crep[b2][:].rearrange("p (t n) -> p t n", n=NST)
                        .unsqueeze(2).broadcast_to([128, W, QD, NST]),
                        op=ALU.mult))
                    run("V", lambda e, b3=b3: e.tensor_reduce(
                        out=ysw[:], in_=amb[b3][:].rearrange("p (tq n) -> p tq n",
                                                             n=NST),
                        axis=AX.X, op=ALU.add))
                    hw = 0 if w < 17 else 1
                    dep("V", "s", szh_map[(c, hw)])
                    if w == 17:
                        cnf, vvf = ygb_wfree
                        dep("V", cnf, vvf)
                    vr = run("V", lambda e, w=w: e.tensor_tensor(
                        out=xsb[:].rearrange("p (q j) -> p q j", j=TC)
                        [:, :, w * W:(w + 1) * W].rearrange("p q j -> p j q"),
                        in0=ysw[:].rearrange("p (t q) -> p t q", q=QD),
                        in1=szh[:].rearrange("p (q j) -> p q j", j=TC)
                        [:, :, w * W:(w + 1) * W].rearrange("p q j -> p j q"),
                        op=ALU.mult), inc="v")
                    win_vfree[g] = ("v", vr)
                    if w == 16:
                        gate_map[(c, 0)] = vr
                    if w == NWIN - 1:
                        gate_map[(c, 1)] = vr
                    nw, ncc = w + 2, c
                    if nw >= NWIN:
                        nw -= NWIN
                        ncc += 1
                    if ncc < NCH:
                        amq.append(emit_am2(ncc, nw))
                        uq.append(emit_u2(ncc, nw))
                    gw[0] += 1
                    if w == 17:
                        postA = post_half_closures(c, 0)
                        preA = (pre_half_closures(c + 1, 0)
                                if c + 1 < NCH else [None] * 9)
                        order = [postA[0], postA[1], postA[2], postA[3],
                                 postA[4], postA[5], postA[6], preA[0],
                                 preA[1], preA[2], preA[3], postA[7],
                                 postA[8], preA[4], preA[5], preA[6],
                                 preA[7], preA[8]]
                        for f_ in order:
                            if f_ is not None:
                                side.append(f_)
                    for _ in range(2):
                        if side:
                            side.popleft()()
                    nff = 2 if c == 0 else FFN_INTERLEAVE
                    if ffn_ok[0]:
                        for _ in range(nff):
                            if ffn_tasks:
                                ffn_tasks.popleft()()
                while ffn_tasks:
                    ffn_tasks.popleft()()

            def mk_ln4(cm1):
                bc4c = [0]

                def f_stats():
                    s_fdone = cur("s")
                    emit_ln(lambda: qv(Fb[:], TC), TC, stat_cast,
                            deps=[("t", cur("t")), ("s", s_fdone)])
                    bc4c.append(s_fdone)

                def f_apply():
                    nonlocal out_dma_prev
                    emit_ln_fin()
                    dep("V", "s", bc4c[1])
                    dep("V", "s", G_s[0])
                    dep("V", "t", last_tp[0])
                    rstd4 = pb[5][:, 0:TC].unsqueeze(1).broadcast_to([128, QD, TC])
                    mu4 = pb[4][:, 0:TC].unsqueeze(1).broadcast_to([128, QD, TC])
                    run("V", lambda e, m=mu4: e.tensor_tensor(
                        out=qv(G[:, 0:QD * TC], TC), in0=qv(Fb[:], TC), in1=m,
                        op=ALU.subtract))
                    v_t1 = run("V", lambda e, r=rstd4: e.tensor_tensor(
                        out=qv(G[:, 0:QD * TC], TC), in0=qv(G[:, 0:QD * TC], TC),
                        in1=r, op=ALU.mult), inc="v")
                    G_v[0] = v_t1
                    bank_consumed(5, "v", v_t1)
                    bank_consumed(4, "v", v_t1)
                    for cls_, vv_ in x2_reload_map.get(cm1, []):
                        dep("Q", cls_, vv_)
                    dep("Q", "v", v_t1)
                    rcl = "qci0" if cm1 % 2 == 0 else "qci1"
                    dep("Q", rcl, cur(rcl))
                    rl = run("Q", lambda e, cm1=cm1: e.dma_start(
                        out=Fb[:], in_=x2scr[cm1 % 2]), inc=rcl, inck=16)
                    dep("V", rcl, rl)
                    v_fin = run("V", lambda e: e.tensor_tensor(
                        out=Fb[:], in0=G[:, 0:QD * TC], in1=Fb[:], op=ALU.add),
                        inc="v")
                    G_v[0] = v_fin
                    dep("Q", "v", v_fin)
                    sv = run("Q", lambda e, cm1=cm1: e.dma_start(
                        out=oT.rearrange("(q p) t -> p q t", p=128)
                        [:, :, cm1 * TC:(cm1 + 1) * TC], in_=qv(Fb[:], TC)),
                        inc="st", inck=16)
                    out_dma_prev = ("st", sv)
                return [f_stats, vh(f_apply)]

            if not have_cur:
                if have_prev:
                    while side:
                        side.popleft()()
                    while ffn_tasks:
                        ffn_tasks.popleft()()
                    for f_ in mk_ln4(cm1):
                        f_()
                break

            postB = post_half_closures(c, 1)
            if have_prev:
                ln4s, ln4a = mk_ln4(cm1)
                side.append(postB.pop(0))
                side.append(ln4s)
                side.append(postB.pop(0))
                side.append(ln4a)
            for f_ in postB:
                side.append(f_)
            w1w2s = [w1w2_group(c, list(range(i, i + 8))) for i in range(0, 32, 8)]
            preB = (pre_half_closures(c + 1, 1) if c + 1 < NCH else [None] * 9)
            orderB = [w1w2s[0], w1w2s[1], preB[0], preB[1], w1w2s[2], preB[2],
                      preB[3], w1w2s[3], preB[4], preB[5], preB[6], preB[7],
                      preB[8]]
            for f_ in orderB:
                if f_ is not None:
                    side.append(f_)

        dep("Q", "st", cur("st"))

        @block.sync
        def _(sync):
            for fn in prog["Q"]:
                fn(sync)

        @block.vector
        def _(vector):
            for fn in prog["V"]:
                fn(vector)

        @block.scalar
        def _(scalar):
            for fn in prog["S"]:
                fn(scalar)

        @block.tensor
        def _(tensor):
            for fn in prog["T"]:
                fn(tensor)

        @block.gpsimd
        def _(gpsimd):
            for fn in prog["P"]:
                fn(gpsimd)

    return nc


def _prep_consts(inp):
    cols = np.zeros((128, CK), dtype=np.float32)

    def put(name, arr):
        o = _offs[name]
        cols[:, o:o + arr.shape[1]] = arr
    put("ipb", np.ascontiguousarray(inp["L0_attn_in_proj_b"].reshape(16, 128).T))
    cw = inp["L0_attn_conv_w"][:, 0, :]
    put("convw", np.ascontiguousarray(
        cw.reshape(8, 128, 4).transpose(1, 0, 2).reshape(128, 32)))
    put("convb", np.ascontiguousarray(inp["L0_attn_conv_b"].reshape(8, 128).T))
    put("dtbias", np.ascontiguousarray(inp["L0_attn_dt_b"].reshape(8, 128).T))
    put("outb", np.ascontiguousarray(inp["L0_attn_out_b"].reshape(8, 128).T))
    put("lng", np.ascontiguousarray(inp["L0_attn_ln_g"].reshape(8, 128).T))
    put("lnb", np.ascontiguousarray(inp["L0_attn_ln_beta"].reshape(8, 128).T))
    al = inp["L0_attn_A_log"]
    put("alog", np.ascontiguousarray(
        al.reshape(8, 128, 32).transpose(1, 0, 2).reshape(128, 256)))
    put("w1s", np.ascontiguousarray(inp["L0_ffn_w1_s"].reshape(32, 128).T))
    put("w2s", np.ascontiguousarray(inp["L0_ffn_w2_s"].reshape(32, 128).T))
    put("h1s", np.ascontiguousarray(inp["L0_ffn_h1_s"].reshape(32, 128).T))
    put("h2s", np.ascontiguousarray(inp["L0_ffn_h2_s"].reshape(32, 128).T))
    put("w3s", np.ascontiguousarray(inp["L0_ffn_w3_s"].reshape(8, 128).T))
    bcb = np.concatenate([inp["L0_attn_B_b"], inp["L0_attn_C_b"]])
    put("bcb", np.tile(bcb[None, :], (128, 1)))
    return cols


def kernel(**inp):
    inp = {k: np.asarray(v) for k, v in inp.items()}
    x = inp["L0_attn_x"]
    consts = _prep_consts(inp)
    wT = {
        "ipT": np.ascontiguousarray(inp["L0_attn_in_proj_w"].T),
        "dtT": np.ascontiguousarray(inp["L0_attn_dt_w"].T),
        "outT": np.ascontiguousarray(inp["L0_attn_out_w"].T),
        "bcT": np.ascontiguousarray(
            np.concatenate([inp["L0_attn_B_w"], inp["L0_attn_C_w"]], axis=0).T),
        "w1T": np.ascontiguousarray(inp["L0_ffn_w1_w"].T),
        "w2T": np.ascontiguousarray(inp["L0_ffn_w2_w"].T),
        "h1T": np.ascontiguousarray(inp["L0_ffn_h1_w"].T),
        "h2T": np.ascontiguousarray(inp["L0_ffn_h2_w"].T),
        "w3T": np.ascontiguousarray(inp["L0_ffn_w3_w"].T),
    }
    nc = build_program()
    in_maps = []
    for i in range(8):
        m = {"xT": np.ascontiguousarray(x[i].T), "consts": consts}
        m.update(wT)
        in_maps.append(m)
    res = run_bass_kernel_spmd(nc, in_maps, core_ids=list(range(8)))
    out = np.stack([np.ascontiguousarray(r["oT"].T) for r in res.results], axis=0)
    return out
